# version 04 (final)
# speedup vs baseline: 1.3311x; 1.3287x over previous
import numpy as np
from contextlib import ExitStack
import concourse.bass as bass
import concourse.mybir as mybir
from concourse.bass_utils import run_bass_kernel_spmd

F32 = mybir.dt.float32
BF16 = mybir.dt.bfloat16
AF = mybir.ActivationFunctionType
ALU = mybir.AluOpType
AX = mybir.AxisListType

NCORES = 8
B, T, D = 4, 4096, 1024
TLOC = (B * T) // NCORES
LN_EPS = 1e-5


class Sems:
    def __init__(self, nc, ctx, names):
        self.h = {}
        self.n = {}
        for nm in names:
            self.h[nm] = ctx.enter_context(nc.semaphore(nm))
            self.n[nm] = 0

    def inc(self, instr, name, k=1):
        instr.then_inc(self.h[name], k)
        self.n[name] += k
        return self.n[name]

    def wait(self, eng, name, val):
        if val > 0:
            eng.wait_ge(self.h[name], val)


def emit_ln(nc, block, S, bufs, x_hbm, NT, prev_gates=None, fp32_out=None):
    xln, sq, xf = bufs["xln"], bufs["sq"], bufs["xf"]
    stat, rb, ab = bufs["stat"], bufs["rb"], bufs["ab"]
    onec, oner, pg = bufs["onec"], bufs["oner"], bufs["pg"]
    nch = NT // 512
    prev_gates = prev_gates or {}

    ld0 = S.n["ld"]

    xin = (x_hbm if isinstance(x_hbm, list)
           else [x_hbm[128 * i:128 * (i + 1), :] for i in range(8)])

    @block.gpsimd
    def _(g):
        for nm, v in prev_gates.items():
            S.wait(g, nm, v)
        for i in range(8):
            S.inc(g.dma_start(out=xln[i], in_=xin[i]), "ld", 16)
    ld_x = S.n["ld"]
    assert ld_x == ld0 + 128

    pe0 = S.n["pe"]
    s0 = S.n["ssem"]

    @block.scalar
    def _(s):
        for nm, v in prev_gates.items():
            S.wait(s, nm, v)
        S.wait(s, "ld", ld_x)
        for i in range(8):
            if i >= 2:
                S.wait(s, "pe", pe0 + (i - 1))
            S.inc(s.activation(out=sq[i % 2][:], in_=xln[i], func=AF.Square),
                  "ssem", 1)
    assert S.n["ssem"] == s0 + 8

    @block.tensor
    def _(t):
        for nm, v in prev_gates.items():
            S.wait(t, nm, v)
        S.wait(t, "gsem", 1)
        S.wait(t, "ld", ld_x)
        for i in range(8):
            for ch in range(nch):
                sl = slice(512 * ch, 512 * (ch + 1))
                t.matmul(pg[0][0:1, sl], onec[:], xln[i][:, sl],
                         start=(i == 0), stop=(i == 7))
            S.wait(t, "ssem", s0 + i + 1)
            last = None
            for ch in range(nch):
                sl = slice(512 * ch, 512 * (ch + 1))
                last = t.matmul(pg[1][0:1, sl], onec[:], sq[i % 2][:, sl],
                                start=(i == 0), stop=(i == 7))
            S.inc(last, "pe", 1)
    pe_sums = S.n["pe"]
    assert pe_sums == pe0 + 8

    mu = stat[:, 0:NT]
    m2 = stat[:, NT:2 * NT]
    var = m2
    rr = m2
    aa = stat[:, 2 * NT:3 * NT]

    v0 = S.n["vsem"]

    @block.vector
    def _(v):
        S.wait(v, "pe", pe_sums)
        v.tensor_scalar(mu, pg[0][0:1, 0:NT], 1.0 / D, None, ALU.mult)
        v.tensor_scalar(m2, pg[1][0:1, 0:NT], 1.0 / D, None, ALU.mult)
        v.tensor_tensor(aa, mu, mu, ALU.mult)
        v.tensor_tensor(var, m2, aa, ALU.subtract)
        S.inc(v.tensor_scalar(var, var, LN_EPS, None, ALU.add), "vsem", 1)
    v_var = S.n["vsem"]

    s_r0 = S.n["ssem"]

    @block.scalar
    def _(s):
        S.wait(s, "vsem", v_var)
        s.activation(out=var, in_=var, func=AF.Ln)
        S.inc(s.activation(out=rr, in_=var, func=AF.Exp, scale=-0.5), "ssem", 1)
    s_r = S.n["ssem"]

    @block.vector
    def _(v):
        S.wait(v, "ssem", s_r)
        S.inc(v.scalar_tensor_tensor(aa, mu, -1.0, rr, ALU.mult, ALU.mult), "vsem", 1)
    v_a = S.n["vsem"]

    @block.tensor
    def _(t):
        S.wait(t, "vsem", v_a)
        last = None
        for ch in range(nch):
            sl = slice(512 * ch, 512 * (ch + 1))
            t.matmul(pg[2][:, sl], oner[:], rr[:, sl], start=True, stop=True)
            last = t.matmul(pg[3][:, sl], oner[:], aa[:, sl], start=True, stop=True)
        S.inc(last, "pe", 1)
    pe_bc = S.n["pe"]

    @block.vector
    def _(v):
        S.wait(v, "pe", pe_bc)
        v.tensor_copy(rb[:], pg[2][:, 0:NT])
        S.inc(v.tensor_copy(ab[:], pg[3][:, 0:NT]), "vsem", 1)
    v_bc = S.n["vsem"]

    ldp0 = [S.n["lda"], S.n["ldb"]]

    @block.gpsimd
    def _(g):
        for i in range(8):
            if i >= 2:
                S.wait(g, "vsem", v_bc + (i - 1))
            S.inc(g.dma_start(out=xf[i % 2][:], in_=xin[i]),
                  "lda" if i % 2 == 0 else "ldb", 16)

    @block.vector
    def _(v):
        S.wait(v, "vsem", v_bc)
        for i in range(8):
            S.wait(v, "lda" if i % 2 == 0 else "ldb", ldp0[i % 2] + 16 * (i // 2 + 1))
            v.tensor_tensor(xf[i % 2][:], xf[i % 2][:], rb[:], ALU.mult)
            out_i = fp32_out[i] if fp32_out is not None else xln[i]
            S.inc(v.tensor_tensor(out_i, xf[i % 2][:], ab[:], ALU.add), "vsem", 1)
    v_norm = S.n["vsem"]
    assert v_norm == v_bc + 8
    return v_norm


def emit_dense(nc, block, S, *, n_ot, nkt, nch, streams, src, wb, pg,
               evac, tensor_pre_waits=None, dma_pre_gate=None):
    nw = len(streams)
    wdp0 = [S.n["wd0"], S.n["wd1"]]
    vf0 = S.n["vfree"]
    pe0 = S.n["pe"]
    per_ot_wd = sum((nkt if (st["transpose"] or st.get("tiled")) else 1)
                    for st in streams) * 16

    @block.sync
    def _(sy):
        if dma_pre_gate:
            for nm, v in dma_pre_gate.items():
                S.wait(sy, nm, v)
        for ot in range(n_ot):
            if ot >= 2:
                S.wait(sy, "pe", pe0 + nw * (ot - 1))
            wsem = "wd0" if ot % 2 == 0 else "wd1"
            for w, st in enumerate(streams):
                buf = wb[2 * w + ot % 2]
                if st["transpose"]:
                    for kt in range(nkt):
                        S.inc(sy.dma_start(out=buf[:, 128 * kt:128 * (kt + 1)],
                                           in_=st["hbm"][128 * ot:128 * (ot + 1),
                                                         128 * kt:128 * (kt + 1)],
                                           transpose=True), wsem, 16)
                elif st.get("tiled"):
                    for kt in range(nkt):
                        S.inc(sy.dma_start(out=buf[:, 128 * kt:128 * (kt + 1)],
                                           in_=st["hbm"][128 * kt:128 * (kt + 1),
                                                         128 * ot:128 * (ot + 1)]),
                              wsem, 16)
                else:
                    S.inc(sy.dma_start(out=buf[:, 0:128 * nkt],
                                       in_=st["hbm"][ot]), wsem, 16)

    @block.tensor
    def _(t):
        if tensor_pre_waits:
            for nm, v in tensor_pre_waits.items():
                S.wait(t, nm, v)
        for ot in range(n_ot):
            S.wait(t, "wd0" if ot % 2 == 0 else "wd1",
                   wdp0[ot % 2] + per_ot_wd * (ot // 2 + 1))
            if ot >= 2:
                S.wait(t, "vfree", vf0 + (ot - 1))
            for w in range(nw):
                psum = pg[2 * w + ot % 2]
                buf = wb[2 * w + ot % 2]
                last = None
                for kt in range(nkt):
                    for ch in range(nch):
                        sl = slice(512 * ch, 512 * (ch + 1))
                        last = t.matmul(psum[:, sl], buf[:, 128 * kt:128 * (kt + 1)],
                                        src(kt, ch), start=(kt == 0), stop=(kt == nkt - 1),
                                        skip_group_check=True)
                S.inc(last, "pe", 1)
    assert S.n["pe"] == pe0 + nw * n_ot
    evac(pe0, vf0)
    assert S.n["vfree"] == vf0 + n_ot, f"evac must inc vfree once per ot"
    return pe0, vf0


def build_l0ffn(dbg_stop=None, dbg_halves=2, pre_t=False):
    nc = bass.Bass("TRN2", target_bir_lowering=False, detect_race_conditions=False)
    xT = nc.dram_tensor("xT", [D, TLOC], F32, kind="ExternalInput").ap()
    if pre_t:
        w1b = nc.dram_tensor("w1b", [32, 128, 1024], BF16, kind="ExternalInput").ap()
        w2b = nc.dram_tensor("w2b", [32, 128, 1024], BF16, kind="ExternalInput").ap()
        h1b = nc.dram_tensor("h1b", [32, 128, 4096], BF16, kind="ExternalInput").ap()
        h2b = nc.dram_tensor("h2b", [32, 128, 4096], BF16, kind="ExternalInput").ap()
        w3b = nc.dram_tensor("w3b", [8, 128, 4096], BF16, kind="ExternalInput").ap()
    else:
        w1b = nc.dram_tensor("w1b", [4096, 1024], BF16, kind="ExternalInput").ap()
        w2b = nc.dram_tensor("w2b", [4096, 1024], BF16, kind="ExternalInput").ap()
        h1b = nc.dram_tensor("h1b", [4096, 4096], BF16, kind="ExternalInput").ap()
        h2b = nc.dram_tensor("h2b", [4096, 4096], BF16, kind="ExternalInput").ap()
        w3b = nc.dram_tensor("w3b", [1024, 4096], BF16, kind="ExternalInput").ap()
    s1 = nc.dram_tensor("s1", [128, 32], F32, kind="ExternalInput").ap()
    s2 = nc.dram_tensor("s2", [128, 32], F32, kind="ExternalInput").ap()
    sh1 = nc.dram_tensor("sh1", [128, 32], F32, kind="ExternalInput").ap()
    sh2 = nc.dram_tensor("sh2", [128, 32], F32, kind="ExternalInput").ap()
    s3 = nc.dram_tensor("s3", [128, 8], F32, kind="ExternalInput").ap()
    yT = nc.dram_tensor("yT", [D, TLOC], F32, kind="ExternalOutput").ap()

    NT = 1024
    nch = NT // 512
    with ExitStack() as ctx:
        sb = lambda n, sh, dt: ctx.enter_context(nc.sbuf_tensor(n, sh, dt))
        ps = lambda n, sh: ctx.enter_context(nc.psum_tensor(n, sh, F32))
        xln = [sb(f"xln{i}", [128, NT], BF16) for i in range(8)]
        sq = [sb(f"sq{i}", [128, NT], BF16) for i in range(2)]
        xf = [sb(f"xf{i}", [128, NT], F32) for i in range(2)]
        hsig = [sb(f"hsig{o}", [128, NT], BF16) for o in range(32)]
        hg = [sb(f"hg{o}", [128, NT], BF16) for o in range(32)]
        wb = [sb("wb0", [128, 4096], BF16), sb("wb1", [128, 4096], BF16),
              sb("wb2", [128, 1024], BF16), sb("wb3", [128, 1024], BF16)]
        rb = sb("rb", [128, NT], F32)
        ab = sb("ab", [128, NT], F32)
        stat = sb("stat", [1, 3 * NT], F32)
        scl = {}
        for nm, hb, w_ in [("s1", s1, 32), ("s2", s2, 32), ("sh1", sh1, 32),
                           ("sh2", sh2, 32), ("s3", s3, 8)]:
            scl[nm] = (sb("scl_" + nm, [128, w_], F32), hb)
        tch = sb("tch", [128, 128], BF16)
        onec = sb("onec", [128, 1], BF16)
        oner = sb("oner", [1, 128], F32)
        fres = [sb(f"fres{i}", [128, NT], F32) for i in range(2)]
        pg = [ps(f"pg{i}", [128, NT]) for i in range(4)]
        tsig = sq
        S = Sems(nc, ctx, ["ld", "lda", "ldb", "pe", "ssem", "vsem", "wd0", "wd1",
                           "gsem", "sta", "stb", "scl", "vfree"])
        block = ctx.enter_context(nc.Block())

        @block.gpsimd
        def _(g):
            g.memset(onec[:], 1.0)
            g.memset(oner[:], 1.0)
            S.inc(g.nop(), "gsem", 1)

        @block.sync
        def _(sy):
            for nm in scl:
                S.inc(sy.dma_start(out=scl[nm][0][:], in_=scl[nm][1]), "scl", 16)
            if dbg_stop is not None:
                for hb in [w1b, w2b, h1b, h2b, w3b]:
                    src = hb[0:1, :, 0:128].squeeze() if pre_t else hb[0:128, 0:128]
                    S.inc(sy.dma_start(out=tch[:], in_=src), "scl", 16)
        wd_scl = S.n["scl"]

        ln_bufs = dict(xln=[t[:] for t in xln], sq=sq, xf=xf, stat=stat, rb=rb,
                       ab=ab, onec=onec, oner=oner, pg=pg)

        prev = {}
        for half in range(dbg_halves):
            t0 = half * NT
            xh = xT[:, t0:t0 + NT]
            yh = yT[:, t0:t0 + NT]

            v_norm = emit_ln(nc, block, S, ln_bufs, xh, NT, prev_gates=prev)

            def dbg_dump(tiles, gate_sem, gate_val, n=8):
                @block.gpsimd
                def _(g, yh=yh):
                    S.wait(g, gate_sem, gate_val)
                    for i in range(n):
                        S.inc(g.dma_start(out=yh[128 * i:128 * (i + 1), :],
                                          in_=tiles[i][:]),
                              "sta" if i % 2 == 0 else "stb", 16)

            if dbg_stop == "ln":
                dbg_dump(xln, "vsem", v_norm)
                prev = {"pe": S.n["pe"], "ssem": S.n["ssem"], "vsem": S.n["vsem"],
                        "vfree": S.n["vfree"], "sta": S.n["sta"], "stb": S.n["stb"]}
                continue

            s12_0 = S.n["ssem"]
            v12_0 = S.n["vsem"]

            def evac12(pe0, vf0):
                @block.scalar
                def _(s):
                    S.wait(s, "scl", wd_scl)
                    for ot in range(32):
                        S.wait(s, "pe", pe0 + 2 * ot + 1)
                        if ot >= 2:
                            S.wait(s, "vfree", vf0 + (ot - 1))
                        S.inc(s.activation(out=tsig[ot % 2][:], in_=pg[ot % 2][:, 0:NT],
                                           func=AF.Silu,
                                           scale=scl["s1"][0][:, ot:ot + 1]), "ssem", 1)

                @block.vector
                def _(v):
                    for ot in range(32):
                        S.wait(v, "pe", pe0 + 2 * ot + 2)
                        S.wait(v, "ssem", s12_0 + ot + 1)
                        S.inc(v.scalar_tensor_tensor(hsig[ot][:],
                                                     pg[2 + ot % 2][:, 0:NT],
                                                     scl["s2"][0][:, ot:ot + 1],
                                                     tsig[ot % 2][:],
                                                     ALU.mult, ALU.mult), "vfree", 1)

            emit_dense(
                nc, block, S, n_ot=32, nkt=8, nch=nch,
                streams=[{"hbm": w1b, "transpose": not pre_t},
                         {"hbm": w2b, "transpose": not pre_t}],
                src=lambda kt, ch: xln[kt][:, 512 * ch:512 * (ch + 1)],
                wb=wb, pg=pg, evac=evac12,
                tensor_pre_waits={"vsem": v_norm},
                dma_pre_gate={"pe": S.n["pe"]})

            if dbg_stop == "w12":
                dbg_dump(hsig, "vfree", S.n["vfree"])
                prev = {"pe": S.n["pe"], "ssem": S.n["ssem"], "vsem": S.n["vsem"],
                        "vfree": S.n["vfree"], "sta": S.n["sta"], "stb": S.n["stb"]}
                continue

            def evac_h(dst, sclcol):
                def f(pe0, vf0):
                    @block.scalar
                    def _(s):
                        for ot in range(32):
                            S.wait(s, "pe", pe0 + ot + 1)
                            S.inc(s.activation(out=dst[ot][:], in_=pg[ot % 2][:, 0:NT],
                                               func=AF.Gelu,
                                               scale=sclcol[:, ot:ot + 1]), "vfree", 1)
                return f

            emit_dense(nc, block, S, n_ot=32, nkt=32, nch=nch,
                       streams=[{"hbm": h1b, "transpose": not pre_t}],
                       src=lambda kt, ch: hsig[kt][:, 512 * ch:512 * (ch + 1)],
                       wb=wb, pg=pg, evac=evac_h(hg, scl["sh1"][0]),
                       tensor_pre_waits={"vfree": S.n["vfree"]},
                       dma_pre_gate={"pe": S.n["pe"]})

            if dbg_stop == "h1":
                dbg_dump(hg, "vfree", S.n["vfree"])
                prev = {"pe": S.n["pe"], "ssem": S.n["ssem"], "vsem": S.n["vsem"],
                        "vfree": S.n["vfree"], "sta": S.n["sta"], "stb": S.n["stb"]}
                continue

            emit_dense(nc, block, S, n_ot=32, nkt=32, nch=nch,
                       streams=[{"hbm": h2b, "transpose": not pre_t}],
                       src=lambda kt, ch: hg[kt][:, 512 * ch:512 * (ch + 1)],
                       wb=wb, pg=pg, evac=evac_h(hsig, scl["sh2"][0]),
                       tensor_pre_waits={"vfree": S.n["vfree"]},
                       dma_pre_gate={"pe": S.n["pe"]})

            if dbg_stop == "h2":
                dbg_dump(hsig, "vfree", S.n["vfree"])
                prev = {"pe": S.n["pe"], "ssem": S.n["ssem"], "vsem": S.n["vsem"],
                        "vfree": S.n["vfree"], "sta": S.n["sta"], "stb": S.n["stb"]}
                continue

            st_0 = [S.n["sta"], S.n["stb"]]
            ldr0 = [S.n["lda"], S.n["ldb"]]
            vf_w3 = S.n["vfree"]

            s3_0 = S.n["ssem"]

            def evac3(pe0, vf0):
                @block.scalar
                def _(s):
                    for ot in range(8):
                        S.wait(s, "pe", pe0 + ot + 1)
                        S.wait(s, "sta" if ot % 2 == 0 else "stb",
                               st_0[ot % 2] + 16 * (ot // 2))
                        S.inc(s.activation(out=fres[ot % 2][:], in_=pg[ot % 2][:, 0:NT],
                                           func=AF.Copy,
                                           scale=scl["s3"][0][:, ot:ot + 1]), "ssem", 1)

                @block.vector
                def _(v):
                    for ot in range(8):
                        S.wait(v, "ssem", s3_0 + ot + 1)
                        S.wait(v, "lda" if ot % 2 == 0 else "ldb",
                               ldr0[ot % 2] + 16 * (ot // 2 + 1))
                        S.inc(v.tensor_tensor(fres[ot % 2][:], fres[ot % 2][:],
                                              xf[ot % 2][:], ALU.add), "vfree", 1)

            emit_dense(nc, block, S, n_ot=8, nkt=32, nch=nch,
                       streams=[{"hbm": w3b, "transpose": not pre_t}],
                       src=lambda kt, ch: hsig[kt][:, 512 * ch:512 * (ch + 1)],
                       wb=wb, pg=pg, evac=evac3,
                       tensor_pre_waits={"vfree": vf_w3},
                       dma_pre_gate={"pe": S.n["pe"]})

            @block.gpsimd
            def _(g, yh=yh, xh=xh):
                S.wait(g, "vsem", v_norm)
                for ot in range(8):
                    S.inc(g.dma_start(out=xf[ot % 2][:],
                                      in_=xh[128 * ot:128 * (ot + 1), :]),
                          "lda" if ot % 2 == 0 else "ldb", 16)
                    S.wait(g, "vfree", vf_w3 + ot + 1)
                    S.inc(g.dma_start(out=yh[128 * ot:128 * (ot + 1), :],
                                      in_=fres[ot % 2][:]),
                          "sta" if ot % 2 == 0 else "stb", 16)

            prev = {"pe": S.n["pe"], "ssem": S.n["ssem"], "vsem": S.n["vsem"],
                    "vfree": S.n["vfree"], "sta": S.n["sta"], "stb": S.n["stb"]}

        @block.gpsimd
        def _(g):
            S.wait(g, "sta", S.n["sta"])
            S.wait(g, "stb", S.n["stb"])

    return nc


def kernel(**inputs):
    raise NotImplementedError


def build_l0attna():
    nc = bass.Bass("TRN2", target_bir_lowering=False, detect_race_conditions=False)
    xTb = nc.dram_tensor("xTb", [D, T], F32, kind="ExternalInput").ap()
    wqkvb = nc.dram_tensor("wqkvb", [1024, 3072], BF16, kind="ExternalInput").ap()
    woutb = nc.dram_tensor("woutb", [1024, 1024], BF16, kind="ExternalInput").ap()
    bout = nc.dram_tensor("bout", [128, 8], F32, kind="ExternalInput").ap()
    bqkvc = nc.dram_tensor("bqkvc", [64, 48], F32, kind="ExternalInput").ap()
    idn = nc.dram_tensor("idn", [128, 128], BF16, kind="ExternalInput").ap()
    yT = nc.dram_tensor("yT", [D, TLOC], F32, kind="ExternalOutput").ap()

    NT = 1024
    with ExitStack() as ctx:
        sb = lambda n, sh, dt: ctx.enter_context(nc.sbuf_tensor(n, sh, dt))
        xlnb = [sb(f"xlnb{i}", [128, T], BF16) for i in range(8)]
        sq = [sb(f"sq{i}", [128, NT], BF16) for i in range(2)]
        xf = [sb(f"xf{i}", [128, NT], F32) for i in range(2)]
        rb = sb("rb", [128, NT], F32)
        ab = sb("ab", [128, NT], F32)
        stat = sb("stat", [1, 3 * NT], F32)
        onec = sb("onec", [128, 1], BF16)
        oner = sb("oner", [1, 128], F32)
        on32 = sb("on32", [1, 128], F32)
        bcol = sb("bcol", [128, 8], F32)
        bqk = sb("bqk", [64, 48], F32)
        sidn = sb("sidn", [128, 128], BF16)
        wqh = sb("wqh", [128, 512], BF16)
        wkh = sb("wkh", [128, 512], BF16)
        wvh = sb("wvh", [128, 512], BF16)
        qh = sb("qh", [64, TLOC], BF16)
        kh = sb("kh", [64, T], BF16)
        vt = sb("vt", [64, T], BF16)
        vvt = sb("vvt", [128, 65 * 32], BF16)
        pt = [sb(f"pt{i}", [128, 512], BF16) for i in range(2)]
        ao = [sb(f"ao{i}", [128, TLOC], BF16) for i in range(8)]
        rrow = sb("rrow", [1, 512], F32)
        fres = [sb(f"fres{i}", [128, TLOC], F32) for i in range(2)]
        wb = [sb(f"wbo{i}", [128, 1024], BF16) for i in range(2)]
        S = Sems(nc, ctx, ["ld", "lda", "ldb", "pe", "pes", "pvs", "ssem", "vsem",
                           "wd0", "wd1", "gsem", "sta", "stb", "scl", "vfree"])
        block = ctx.enter_context(nc.Block())

        @block.gpsimd
        def _(g):
            g.memset(onec[:], 1.0)
            g.memset(oner[:], 1.0)
            g.memset(on32[:], 1.0 / 32.0)
            g.memset(vvt[:], 1.0)
            S.inc(g.nop(), "gsem", 1)

        @block.sync
        def _(sy):
            S.inc(sy.dma_start(out=bcol[:], in_=bout), "scl", 16)
            S.inc(sy.dma_start(out=sidn[:], in_=idn), "scl", 16)
            S.inc(sy.dma_start(out=bqk[:], in_=bqkvc), "scl", 16)
        scl_done = S.n["scl"]

        with ExitStack() as lctx:
            lpg = [lctx.enter_context(nc.psum_tensor(f"lpg{i}", [128, NT], F32))
                   for i in range(4)]
            prev = {}
            for qtr in range(4):
                bufs = dict(xln=[xlnb[i][:, NT * qtr:NT * (qtr + 1)] for i in range(8)],
                            sq=sq, xf=xf, stat=stat, rb=rb, ab=ab,
                            onec=onec, oner=oner, pg=lpg)
                v_norm = emit_ln(nc, block, S, bufs, xTb[:, NT * qtr:NT * (qtr + 1)],
                                 NT, prev_gates=prev)
                prev = {"pe": S.n["pe"], "ssem": S.n["ssem"], "vsem": S.n["vsem"]}
        ln_done = dict(prev)

        with ExitStack() as actx:
            psQK = [actx.enter_context(nc.psum_tensor(f"psQK{i}", [64, 512], F32))
                    for i in range(2)]
            psS = [actx.enter_context(nc.psum_tensor(f"psS{i}", [128, 512], F32))
                   for i in range(2)]
            psO = actx.enter_context(nc.psum_tensor("psO", [65, 512], F32))
            psBC = actx.enter_context(nc.psum_tensor("psBC", [128, 512], F32))
            psT = [actx.enter_context(nc.psum_tensor(f"psT{i}", [128, 64], BF16))
                   for i in range(2)]

            for h in range(16):
                pe_h0 = S.n["pe"]
                wl0 = S.n["wd0"]

                @block.sync
                def _(sy, h=h, pe_h0=pe_h0):
                    S.wait(sy, "pe", pe_h0)
                    for kt in range(8):
                        for buf, off in [(wqh, 0), (wkh, 1024), (wvh, 2048)]:
                            S.inc(sy.dma_start(
                                out=buf[:, 64 * kt:64 * (kt + 1)],
                                in_=wqkvb[128 * kt:128 * (kt + 1),
                                          off + 64 * h:off + 64 * (h + 1)]), "wd0", 16)
                wl1 = S.n["wd0"]

                jobs = []
                for ch in range(4):
                    jobs.append((wqh, qh[:, 512 * ch:512 * (ch + 1)],
                                 slice(512 * ch, 512 * (ch + 1)), 0 * 16 + h))
                for ch in range(8):
                    jobs.append((wkh, kh[:, 512 * ch:512 * (ch + 1)],
                                 slice(512 * ch, 512 * (ch + 1)), 1 * 16 + h))
                for ch in range(8):
                    jobs.append((wvh, vt[:, 512 * ch:512 * (ch + 1)],
                                 slice(512 * ch, 512 * (ch + 1)), 2 * 16 + h))
                pe_q0 = S.n["pe"]
                se_q0 = S.n["ssem"]

                @block.tensor
                def _(t, jobs=jobs, wl1=wl1, pe_q0=pe_q0, se_q0=se_q0):
                    S.wait(t, "wd0", wl1)
                    for nm, v in ln_done.items():
                        S.wait(t, nm, v)
                    for j, (wsl, dst, sl, bi_) in enumerate(jobs):
                        if j >= 2:
                            S.wait(t, "ssem", se_q0 + (j - 1))
                        last = None
                        for kt in range(8):
                            last = t.matmul(psQK[j % 2][:], wsl[:, 64 * kt:64 * (kt + 1)],
                                            xlnb[kt][:, sl], start=(kt == 0),
                                            stop=(kt == 7), skip_group_check=True)
                        S.inc(last, "pe", 1)

                pes_prev = S.n["pes"]
                pvs_prev = S.n["pvs"]

                @block.scalar
                def _(s, jobs=jobs, pe_q0=pe_q0, pes_prev=pes_prev,
                      pvs_prev=pvs_prev):
                    S.wait(s, "pes", pes_prev)
                    S.wait(s, "pvs", pvs_prev)
                    for j, (wsl, dst, sl, bi_) in enumerate(jobs):
                        S.wait(s, "pe", pe_q0 + j + 1)
                        S.wait(s, "scl", scl_done)
                        S.inc(s.activation(out=dst, in_=psQK[j % 2][:],
                                           func=AF.Identity,
                                           bias=bqk[:, bi_:bi_ + 1]), "ssem", 1)
                se_q1 = S.n["ssem"]

                pe_t0 = S.n["pe"]
                vf_t0 = S.n["vfree"]

                @block.tensor
                def _(t, se_q1=se_q1, pe_t0=pe_t0, vf_t0=vf_t0):
                    S.wait(t, "ssem", se_q1)
                    for tt in range(32):
                        if tt >= 2:
                            S.wait(t, "vfree", vf_t0 + (tt - 1))
                        if tt == 0:
                            S.wait(t, "scl", scl_done)
                        S.inc(t.transpose(psT[tt % 2][0:128, 0:64],
                                          vt[:, 128 * tt:128 * (tt + 1)],
                                          sidn[0:64, 0:64]), "pe", 1)

                pvs_prev2 = S.n["pvs"]

                @block.vector
                def _(v, pe_t0=pe_t0, pvs_prev2=pvs_prev2):
                    S.wait(v, "pvs", pvs_prev2)
                    for tt in range(32):
                        S.wait(v, "pe", pe_t0 + tt + 1)
                        S.inc(v.tensor_copy(vvt[:, 65 * tt:65 * tt + 64],
                                            psT[tt % 2][:]), "vfree", 1)
                vf_t1 = S.n["vfree"]

                for qc in range(4):
                    pes0 = S.n["pes"]
                    pvs0 = S.n["pvs"]
                    se0 = S.n["ssem"]
                    qsl = slice(512 * qc, 512 * (qc + 1))

                    vs_prev = S.n["vsem"]

                    @block.tensor
                    def _(t, pes0=pes0, pvs0=pvs0, se0=se0, qsl=qsl, vf_t1=vf_t1,
                          h=h, qc=qc, vs_prev=vs_prev):
                        if qc == 0:
                            S.wait(t, "vfree", vf_t1)
                        S.wait(t, "vsem", vs_prev)
                        order = []
                        for kt in range(32):
                            order.append(("s", kt))
                            if kt >= 1:
                                order.append(("p", kt - 1))
                        order.append(("p", 31))
                        for kind, kt in order:
                            if kind == "s":
                                if kt >= 2:
                                    S.wait(t, "ssem", se0 + (kt - 1))
                                S.inc(t.matmul(psS[kt % 2][:],
                                               kh[:, 128 * kt:128 * (kt + 1)],
                                               qh[:, qsl], start=True, stop=True,
                                               skip_group_check=True), "pes", 1)
                            else:
                                S.wait(t, "ssem", se0 + kt + 1)
                                S.inc(t.matmul(psO[:], vvt[:, 65 * kt:65 * (kt + 1)],
                                               pt[kt % 2][:], start=(kt == 0),
                                               stop=(kt == 31),
                                               skip_group_check=True), "pvs", 1)

                    @block.scalar
                    def _(s, pes0=pes0, pvs0=pvs0, se0=se0):
                        for kt in range(32):
                            S.wait(s, "pes", pes0 + kt + 1)
                            if kt < 2:
                                S.wait(s, "pvs", pvs0)
                            else:
                                S.wait(s, "pvs", pvs0 + (kt - 1))
                            S.inc(s.activation(out=pt[kt % 2][:], in_=psS[kt % 2][:],
                                               func=AF.Exp), "ssem", 1)

                    pvs1 = S.n["pvs"]
                    vb0 = S.n["vsem"]
                    sb0 = S.n["ssem"]
                    aosl = ao[h // 2][64 * (h % 2):64 * (h % 2) + 64, qsl]

                    @block.scalar
                    def _(s, pvs1=pvs1, aosl=aosl):
                        S.wait(s, "pvs", pvs1)
                        S.inc(s.activation(out=aosl, in_=psO[0:64, :],
                                           func=AF.Copy), "ssem", 1)

                    @block.vector
                    def _(v, pvs1=pvs1, sb0=sb0):
                        S.wait(v, "pvs", pvs1)
                        S.inc(v.reciprocal(rrow[:], psO[64:65, :]), "vsem", 1)

                    vb1 = S.n["vsem"]
                    peb0 = S.n["pe"]

                    vf_prev = S.n["vfree"]

                    @block.tensor
                    def _(t, vb1=vb1, vf_prev=vf_prev):
                        S.wait(t, "vsem", vb1)
                        S.wait(t, "vfree", vf_prev)
                        S.inc(t.matmul(psBC[:], on32[:], rrow[:], start=True,
                                       stop=True, skip_group_check=True), "pe", 1)

                    @block.vector
                    def _(v, peb0=peb0, sb0=sb0, aosl=aosl, h=h):
                        S.wait(v, "pe", peb0 + 1)
                        S.wait(v, "ssem", sb0 + 1)
                        h2 = 64 * (h % 2)
                        S.inc(v.tensor_tensor(aosl, aosl, psBC[h2:h2 + 64, :],
                                              ALU.mult), "vfree", 1)


        st_0 = [S.n["sta"], S.n["stb"]]
        ldr0 = [S.n["lda"], S.n["ldb"]]
        vf_w3 = S.n["vfree"]
        s3_0 = S.n["ssem"]
        aodone = {"vfree": S.n["vfree"], "ssem": S.n["ssem"]}

        def evac_o(pe0, vf0):
            @block.scalar
            def _(s):
                S.wait(s, "scl", scl_done)
                for ot in range(8):
                    S.wait(s, "pe", pe0 + ot + 1)
                    if ot >= 2:
                        S.wait(s, "sta" if ot % 2 == 0 else "stb",
                               st_0[ot % 2] + 16 * (ot // 2))
                    S.inc(s.activation(out=fres[ot % 2][:], in_=pg2[ot % 2][:],
                                       func=AF.Copy), "ssem", 1)

            @block.vector
            def _(v):
                for ot in range(8):
                    S.wait(v, "ssem", s3_0 + ot + 1)
                    S.wait(v, "lda" if ot % 2 == 0 else "ldb",
                           ldr0[ot % 2] + 16 * (ot // 2 + 1))
                    v.tensor_scalar(fres[ot % 2][:], fres[ot % 2][:],
                                    bcol[:, ot:ot + 1], None, ALU.add)
                    S.inc(v.tensor_tensor(fres[ot % 2][:], fres[ot % 2][:],
                                          xq[ot % 2][:], ALU.add), "vfree", 1)

        with ExitStack() as octx:
            pg2 = [octx.enter_context(nc.psum_tensor(f"pg2{i}", [128, TLOC], F32))
                   for i in range(2)]
            xq = [octx.enter_context(nc.sbuf_tensor(f"xq{i}", [128, TLOC], F32))
                  for i in range(2)]

            @block.gpsimd
            def _(g):
                for ot in range(8):
                    S.inc(g.dma_start(out=xq[ot % 2][:],
                                      in_=xTb[128 * ot:128 * (ot + 1), 0:TLOC]),
                          "lda" if ot % 2 == 0 else "ldb", 16)
                    S.wait(g, "vfree", vf_w3 + ot + 1)
                    S.inc(g.dma_start(out=yT[128 * ot:128 * (ot + 1), :],
                                      in_=fres[ot % 2][:]),
                          "sta" if ot % 2 == 0 else "stb", 16)

            emit_dense(nc, block, S, n_ot=8, nkt=8, nch=4,
                       streams=[{"hbm": woutb, "transpose": False, "tiled": True}],
                       src=lambda kt, ch: ao[kt][:, 512 * ch:512 * (ch + 1)],
                       wb=wb, pg=pg2, evac=evac_o,
                       tensor_pre_waits=aodone,
                       dma_pre_gate=None)

        @block.gpsimd
        def _(g):
            S.wait(g, "sta", S.n["sta"])
            S.wait(g, "stb", S.n["stb"])

    return nc


def build_l1attnb(q0):
    nc = bass.Bass("TRN2", target_bir_lowering=False, detect_race_conditions=False)
    xTb = nc.dram_tensor("xTb", [D, T], F32, kind="ExternalInput").ap()
    wqb = nc.dram_tensor("wqb", [1024, 1024], BF16, kind="ExternalInput").ap()
    wkvb = nc.dram_tensor("wkvb", [1024, 128], BF16, kind="ExternalInput").ap()
    wkbp = nc.dram_tensor("wkbp", [128, 1024], BF16, kind="ExternalInput").ap()
    wvb = nc.dram_tensor("wvb", [128, 512], BF16, kind="ExternalInput").ap()
    wob = nc.dram_tensor("wob", [512, 1024], BF16, kind="ExternalInput").ap()
    qgcol = nc.dram_tensor("qgcol", [128, 1], F32, kind="ExternalInput").ap()
    kgcol = nc.dram_tensor("kgcol", [128, 1], F32, kind="ExternalInput").ap()
    bsel_q = nc.dram_tensor("bsel_q", [128, 2], BF16, kind="ExternalInput").ap()
    rep_q = nc.dram_tensor("rep_q", [2, 128], F32, kind="ExternalInput").ap()
    bsel_k = nc.dram_tensor("bsel_k", [128, 2], BF16, kind="ExternalInput").ap()
    rep_k = nc.dram_tensor("rep_k", [2, 128], F32, kind="ExternalInput").ap()
    qgrow = nc.dram_tensor("qgrow", [1, 64], F32, kind="ExternalInput").ap()
    kgrow = nc.dram_tensor("kgrow", [1, 32], F32, kind="ExternalInput").ap()
    masksh = nc.dram_tensor("masksh", [4, 128, 512], BF16, kind="ExternalInput").ap()
    yT = nc.dram_tensor("yT", [D, TLOC], F32, kind="ExternalOutput").ap()

    NT = 1024
    RMS_EPS = 1e-6

    with ExitStack() as ctx:
        sb = lambda n, sh, dt: ctx.enter_context(nc.sbuf_tensor(n, sh, dt))
        xlnb = [sb(f"xlnb{i}", [128, T], BF16) for i in range(8)]
        ktp = xlnb
        sq = [sb(f"sq{i}", [128, NT], BF16) for i in range(2)]
        xf = [sb(f"xf{i}", [128, NT], F32) for i in range(2)]
        rb = sb("rb", [128, NT], F32)
        ab = sb("ab", [128, NT], F32)
        stat = sb("stat", [1, 3 * NT], F32)
        onec = sb("onec", [128, 1], BF16)
        oner = sb("oner", [1, 128], F32)
        kvt = sb("kvt", [128, T], BF16)
        qh = [sb(f"qh{i}", [128, TLOC], BF16) for i in range(8)]
        vvt = [sb(f"vvt{i}", [128, 528], BF16) for i in range(32)]
        pt = [sb(f"pt{i}", [128, 512], BF16) for i in range(2)]
        ao = [sb(f"ao{i}", [128, TLOC], BF16) for i in range(4)]
        rrow = sb("rrow", [1, 512], F32)
        fres = None
        wbo = [sb(f"wbo{i}", [128, 512], BF16) for i in range(2)]
        wsb2 = [sb(f"wsb2{i}", [128, 1024], BF16) for i in range(2)]
        sqb = [sb(f"sqb{i}", [128, 512], BF16) for i in range(2)]
        msb0 = sb("msb0", [2, 512], F32)
        msb = [msb0, msb0]
        r2 = [sb(f"r2_{i}", [2, 512], F32) for i in range(2)]
        qg = sb("qg", [128, 1], F32)
        kg = sb("kg", [128, 1], F32)
        bselq = sb("bselq", [128, 2], BF16)
        bselk = sb("bselk", [128, 2], BF16)
        repq = sb("repq", [2, 128], F32)
        repk = sb("repk", [2, 128], F32)
        grow = sb("grow", [1, 96], F32)
        negm = sb("negm", [128, 1], F32)
        mtmp = sb("mtmp", [1, 4], F32)
        S = Sems(nc, ctx, ["ld", "lda", "ldb", "pe", "pes", "pvs", "ssem", "sact",
                           "vsem", "wd0", "wd1", "gsem", "sta", "stb", "scl",
                           "vfree"])
        block = ctx.enter_context(nc.Block())

        @block.gpsimd
        def _(g):
            g.memset(onec[:], 1.0)
            g.memset(oner[:], 1.0)
            for tt in range(32):
                g.memset(vvt[tt][:], 1.0)
            S.inc(g.nop(), "gsem", 1)

        @block.sync
        def _(sy):
            for dst, src in [(qg[:], qgcol), (kg[:], kgcol),
                             (bselq[:], bsel_q), (bselk[:], bsel_k),
                             (repq[:], rep_q), (repk[:], rep_k),
                             (grow[:, 0:64], qgrow), (grow[:, 64:96], kgrow)]:
                S.inc(sy.dma_start(out=dst, in_=src), "scl", 16)
        scl_done = S.n["scl"]

        @block.vector
        def _(v):
            S.wait(v, "scl", scl_done)
            v.tensor_reduce(mtmp[:, 0:1], grow[:, 0:64], AX.X, ALU.max,
                            apply_absolute_value=True)
            v.tensor_reduce(mtmp[:, 1:2], grow[:, 64:96], AX.X, ALU.max,
                            apply_absolute_value=True)
            v.tensor_tensor(mtmp[:, 2:3], mtmp[:, 0:1], mtmp[:, 1:2], ALU.mult)
            S.inc(v.tensor_scalar(mtmp[:, 3:4], mtmp[:, 2:3], -5.66, None, ALU.mult),
                  "vsem", 1)
        v_m1 = S.n["vsem"]

        g_m = None

        with ExitStack() as lctx:
            lpg = [lctx.enter_context(nc.psum_tensor(f"lpg{i}", [128, NT], F32))
                   for i in range(4)]
            prev = {}
            for qtr in range(4):
                bufs = dict(xln=[xlnb[i][:, NT * qtr:NT * (qtr + 1)] for i in range(8)],
                            sq=sq, xf=xf, stat=stat, rb=rb, ab=ab,
                            onec=onec, oner=oner, pg=lpg)
                v_norm = emit_ln(nc, block, S, bufs, xTb[:, NT * qtr:NT * (qtr + 1)],
                                 NT, prev_gates=prev)
                prev = {"pe": S.n["pe"], "ssem": S.n["ssem"], "vsem": S.n["vsem"]}
        ln_done = dict(prev)

        with ExitStack() as actx:
            psA = [actx.enter_context(nc.psum_tensor(f"psA{i}", [128, 512], F32))
                   for i in range(2)]
            psS = [actx.enter_context(nc.psum_tensor(f"psS{i}", [128, 512], F32))
                   for i in range(2)]
            psO = actx.enter_context(nc.psum_tensor("psO", [33, 512], F32))
            psB = [actx.enter_context(nc.psum_tensor(f"psB{i}", [128, 512], F32))
                   for i in range(2)]
            psMS = actx.enter_context(nc.psum_tensor("psMS", [2, 512], F32))

            @block.tensor
            def _(t):
                S.wait(t, "vsem", v_m1)
                for nm, v in ln_done.items():
                    S.wait(t, nm, v)
                S.inc(t.matmul(psB[0][:, 0:1], oner[:], mtmp[:, 3:4], start=True,
                               stop=True, skip_group_check=True), "pe", 1)
            pe_m = S.n["pe"]

            @block.scalar
            def _(s):
                S.wait(s, "pe", pe_m)
                S.inc(s.activation(out=negm[:], in_=psB[0][:, 0:1], func=AF.Copy),
                      "gsem", 1)
            g_m = S.n["gsem"]

            wd_kv0 = S.n["wd0"]

            @block.sync
            def _(sy):
                for kt in range(8):
                    S.inc(sy.dma_start(out=wsb2[0][:, 128 * kt:128 * (kt + 1)],
                                       in_=wkvb[128 * kt:128 * (kt + 1), :]),
                          "wd0", 16)
            wd_kv1 = S.n["wd0"]
            pe_kv0 = S.n["pe"]
            se_kv0 = S.n["ssem"]

            @block.tensor
            def _(t):
                S.wait(t, "wd0", wd_kv1)
                for nm, v in ln_done.items():
                    S.wait(t, nm, v)
                for ch in range(8):
                    if ch >= 2:
                        S.wait(t, "ssem", se_kv0 + ch - 1)
                    sl = slice(512 * ch, 512 * (ch + 1))
                    last = None
                    for kt in range(8):
                        last = t.matmul(psA[ch % 2][:],
                                        wsb2[0][:, 128 * kt:128 * (kt + 1)],
                                        xlnb[kt][:, sl], start=(kt == 0),
                                        stop=(kt == 7), skip_group_check=True)
                    S.inc(last, "pe", 1)

            @block.scalar
            def _(s):
                for ch in range(8):
                    S.wait(s, "pe", pe_kv0 + ch + 1)
                    S.inc(s.activation(out=kvt[:, 512 * ch:512 * (ch + 1)],
                                       in_=psA[ch % 2][:], func=AF.Copy), "ssem", 1)
            se_kv1 = S.n["ssem"]

            def emit_proj_rms(w_hbm, w_nkt, n_mo, n_ch, src_tiles, src_base,
                              dst_fn, bsel, rep, gcol, nblk, gate_tensor=None,
                              gate_sync=None):
                wd_0 = [S.n["wd0"], S.n["wd1"]]
                pe_0, pes_0, pvs_0 = S.n["pe"], S.n["pes"], S.n["pvs"]
                se_0, sa_0, vs_0, vf_0 = (S.n["ssem"], S.n["sact"], S.n["vsem"],
                                          S.n["vfree"])
                NJ = n_mo * n_ch

                @block.sync
                def _(sy):
                    if gate_sync:
                        for nm, v in gate_sync.items():
                            S.wait(sy, nm, v)
                    for mo in range(n_mo):
                        if mo >= 2:
                            S.wait(sy, "pe", pe_0 + n_ch * (mo - 1))
                        wsem = "wd0" if mo % 2 == 0 else "wd1"
                        for kt in range(w_nkt):
                            S.inc(sy.dma_start(
                                out=wsb2[mo % 2][:, 128 * kt:128 * (kt + 1)],
                                in_=w_hbm[128 * kt:128 * (kt + 1),
                                          128 * mo:128 * (mo + 1)]), wsem, 16)

                @block.tensor
                def _(t):
                    if gate_tensor:
                        for nm, v in gate_tensor.items():
                            S.wait(t, nm, v)
                    for j in range(NJ):
                        mo, ch = j // n_ch, j % n_ch
                        if ch == 0:
                            S.wait(t, "wd0" if mo % 2 == 0 else "wd1",
                                   wd_0[mo % 2] + 16 * w_nkt * (mo // 2 + 1))
                        if j >= 2:
                            S.wait(t, "ssem", se_0 + (j - 1))
                        sl = slice(src_base + 512 * ch, src_base + 512 * (ch + 1))
                        last = None
                        for kt in range(w_nkt):
                            last = t.matmul(psA[j % 2][:],
                                            wsb2[mo % 2][:, 128 * kt:128 * (kt + 1)],
                                            src_tiles[kt][:, sl], start=(kt == 0),
                                            stop=(kt == w_nkt - 1),
                                            skip_group_check=True)
                        S.inc(last, "pe", 1)
                        if j >= 1:
                            S.wait(t, "ssem", se_0 + j)
                            if j >= 2:
                                S.wait(t, "vsem", vs_0 + j - 1)
                            S.inc(t.matmul(psMS[:], bsel, sqb[(j - 1) % 2][:],
                                           start=True, stop=True,
                                           skip_group_check=True), "pes", 1)
                        if j >= 2:
                            S.wait(t, "sact", sa_0 + j - 1)
                            if j >= 4:
                                S.wait(t, "vfree", vf_0 + j - 3)
                            S.inc(t.matmul(psB[j % 2][:], rep, r2[(j - 2) % 2][:],
                                           start=True, stop=True,
                                           skip_group_check=True), "pvs", 1)
                    S.wait(t, "ssem", se_0 + NJ)
                    S.wait(t, "vsem", vs_0 + NJ - 1)
                    S.inc(t.matmul(psMS[:], bsel, sqb[(NJ - 1) % 2][:], start=True,
                                   stop=True, skip_group_check=True), "pes", 1)
                    for j in (NJ, NJ + 1):
                        S.wait(t, "sact", sa_0 + j - 1)
                        S.wait(t, "vfree", vf_0 + max(0, j - 3))
                        S.inc(t.matmul(psB[j % 2][:], rep, r2[(j - 2) % 2][:],
                                       start=True, stop=True,
                                       skip_group_check=True), "pvs", 1)

                @block.scalar
                def _(s):
                    S.wait(s, "scl", scl_done)
                    for j in range(NJ + 2):
                        if j >= 2:
                            jj = j - 2
                            S.wait(s, "vsem", vs_0 + jj + 1)
                            s.activation(out=msb[jj % 2][:], in_=msb[jj % 2][:],
                                         func=AF.Ln)
                            S.inc(s.activation(out=r2[jj % 2][:], in_=msb[jj % 2][:],
                                               func=AF.Exp, scale=-0.5), "sact", 1)
                        if j < NJ:
                            S.wait(s, "pe", pe_0 + j + 1)
                            if j >= 2:
                                S.wait(s, "vfree", vf_0 + j - 1)
                            s.activation(out=dst_fn(j), in_=psA[j % 2][:],
                                         func=AF.Copy)
                            S.inc(s.activation(out=sqb[j % 2][:], in_=psA[j % 2][:],
                                               func=AF.Square), "ssem", 1)

                @block.vector
                def _(v):
                    for j in range(NJ + 2):
                        if j >= 1 and j <= NJ:
                            jj = j - 1
                            S.wait(v, "pes", pes_0 + jj + 1)
                            if jj >= 1:
                                S.wait(v, "sact", sa_0 + jj)
                            S.inc(v.tensor_scalar(msb[jj % 2][:], psMS[0:2, :],
                                                  1.0 / nblk, RMS_EPS, ALU.mult,
                                                  ALU.add), "vsem", 1)
                        if j >= 2:
                            jj = j - 2
                            S.wait(v, "pvs", pvs_0 + jj + 1)
                            S.inc(v.scalar_tensor_tensor(dst_fn(jj), dst_fn(jj),
                                                         gcol[:, 0:1],
                                                         psB[jj % 2][0:128, :],
                                                         ALU.mult, ALU.mult),
                                  "vfree", 1)
                return NJ

            emit_proj_rms(wqb, 8, 8, 4, xlnb, q0,
                          lambda j: qh[j // 4][:, 512 * (j % 4):512 * (j % 4 + 1)],
                          bselq[:], repq[:], qg, 64,
                          gate_tensor={"gsem": g_m},
                          gate_sync={"pe": S.n["pe"]})
            q_done = {"vfree": S.n["vfree"]}

            emit_proj_rms(wkbp, 1, 8, 8, [kvt], 0,
                          lambda j: ktp[j // 8][:, 512 * (j % 8):512 * (j % 8 + 1)],
                          bselk[:], repk[:], kg, 32,
                          gate_tensor={"vfree": S.n["vfree"], "ssem": S.n["ssem"]},
                          gate_sync={"pe": S.n["pe"]})
            k_done = {"vfree": S.n["vfree"]}

            pe_v0 = S.n["pe"]
            se_v0 = S.n["ssem"]
            wd_v0 = S.n["wd1"]

            pe_preV = S.n["pe"]

            @block.sync
            def _(sy):
                S.wait(sy, "pe", pe_preV)
                S.inc(sy.dma_start(out=wsb2[0][:, 0:512], in_=wvb), "wd1", 16)
            wd_v1 = S.n["wd1"]

            @block.tensor
            def _(t):
                S.wait(t, "wd1", wd_v1)
                S.wait(t, "ssem", se_kv1)
                for tt in range(32):
                    if tt >= 2:
                        S.wait(t, "ssem", se_v0 + tt - 1)
                    S.inc(t.matmul(psA[tt % 2][:], kvt[:, 128 * tt:128 * (tt + 1)],
                                   wsb2[0][:, 0:512], start=True, stop=True,
                                   skip_group_check=True), "pe", 1)

            @block.scalar
            def _(s):
                S.wait(s, "gsem", 1)
                for tt in range(32):
                    S.wait(s, "pe", pe_v0 + tt + 1)
                    S.inc(s.activation(
                        out=vvt[tt][:].rearrange("p (h d) -> p (h d)", h=16) if False
                        else vvt[tt][:, 0:528].rearrange("p (h e) -> p h e", h=16)[:, :, 0:32],
                        in_=psA[tt % 2][:].rearrange("p (h d) -> p h d", h=16),
                        func=AF.Copy), "ssem", 1)
            v_done = S.n["ssem"]
            mtile = kvt[:, 0:2048]
            pe_now = S.n["pe"]

            @block.sync
            def _(sy):
                S.wait(sy, "pe", pe_now)
                for mi in range(4):
                    S.inc(sy.dma_start(out=mtile[:, 512 * mi:512 * (mi + 1)],
                                       in_=masksh[mi]), "scl", 16)
            msk_done = S.n["scl"]

            for h in range(16):
                for qc in range(4):
                    ktmax = q0 // 128 + 4 * (qc + 1)
                    ktd0 = ktmax - 4
                    pes0 = S.n["pes"]
                    pvs0 = S.n["pvs"]
                    se0 = S.n["ssem"]
                    vs_prev = S.n["vsem"]
                    vf_prev = S.n["vfree"]
                    qsl = slice(512 * qc, 512 * (qc + 1))
                    h2 = 64 * (h % 2)
                    kten = ktp[h // 2]

                    vmsk0 = S.n["vsem"]

                    @block.tensor
                    def _(t, pes0=pes0, pvs0=pvs0, se0=se0, qsl=qsl, h=h, qc=qc,
                          ktmax=ktmax, h2=h2, kten=kten, vs_prev=vs_prev,
                          ktd0=ktd0, vmsk0=vmsk0):
                        if h == 0 and qc == 0:
                            S.wait(t, "ssem", v_done)
                            S.wait(t, "vfree", k_done["vfree"])
                        S.wait(t, "vsem", vs_prev)
                        order = []
                        for kt in range(ktmax):
                            order.append(("s", kt))
                            if kt >= 1:
                                order.append(("p", kt - 1))
                        order.append(("p", ktmax - 1))
                        for kind, kt in order:
                            if kind == "s":
                                if kt >= 2:
                                    S.wait(t, "ssem", se0 + (kt - 1))
                                S.inc(t.matmul(psS[kt % 2][:],
                                               kten[h2:h2 + 32, 128 * kt:128 * (kt + 1)],
                                               qh[h // 2][h2:h2 + 32, qsl],
                                               start=True, stop=True,
                                               skip_group_check=True), "pes", 1)
                            else:
                                if kt >= ktd0:
                                    S.wait(t, "vsem", vmsk0 + (kt - ktd0 + 1))
                                else:
                                    S.wait(t, "ssem", se0 + kt + 1)
                                S.inc(t.matmul(psO[:], vvt[kt][:, 33 * h:33 * h + 33],
                                               pt[kt % 2][:], start=(kt == 0),
                                               stop=(kt == ktmax - 1),
                                               skip_group_check=True), "pvs", 1)

                    @block.scalar
                    def _(s, pes0=pes0, pvs0=pvs0, se0=se0, ktmax=ktmax):
                        if h == 0 and qc == 0:
                            S.wait(s, "gsem", g_m)
                        for kt in range(ktmax):
                            S.wait(s, "pes", pes0 + kt + 1)
                            if kt < 2:
                                S.wait(s, "pvs", pvs0)
                            else:
                                S.wait(s, "pvs", pvs0 + (kt - 1))
                            S.inc(s.activation(out=pt[kt % 2][:], in_=psS[kt % 2][:],
                                               func=AF.Exp, scale=0.125,
                                               bias=negm[:, 0:1]), "ssem", 1)

                    vsm0 = S.n["vsem"]

                    @block.vector
                    def _(v, se0=se0, ktd0=ktd0, ktmax=ktmax, pvs0=pvs0, vsm0=vsm0):
                        S.wait(v, "scl", msk_done)
                        for mi, kt in enumerate(range(ktd0, ktmax)):
                            S.wait(v, "ssem", se0 + kt + 1)
                            if kt >= 2:
                                S.wait(v, "pvs", pvs0 + (kt - 1))
                            S.inc(v.tensor_tensor(pt[kt % 2][:], pt[kt % 2][:],
                                                  mtile[:, 512 * mi:512 * (mi + 1)],
                                                  ALU.mult), "vsem", 1)
                    pvs1 = S.n["pvs"]
                    sb0 = S.n["ssem"]
                    h4 = 32 * (h % 4)
                    aosl = ao[h // 4][h4:h4 + 32, qsl]

                    @block.scalar
                    def _(s, pvs1=pvs1, aosl=aosl):
                        S.wait(s, "pvs", pvs1)
                        S.inc(s.activation(out=aosl, in_=psO[0:32, :],
                                           func=AF.Copy), "ssem", 1)

                    @block.vector
                    def _(v, pvs1=pvs1):
                        S.wait(v, "pvs", pvs1)
                        S.inc(v.reciprocal(rrow[:], psO[32:33, :]), "vsem", 1)
                    vb1 = S.n["vsem"]
                    peb0 = S.n["pe"]

                    @block.tensor
                    def _(t, vb1=vb1, vf_prev=vf_prev):
                        S.wait(t, "vsem", vb1)
                        S.wait(t, "vfree", vf_prev)
                        S.inc(t.matmul(psB[0][:], oner[:], rrow[:], start=True,
                                       stop=True, skip_group_check=True), "pe", 1)

                    @block.vector
                    def _(v, peb0=peb0, sb0=sb0, aosl=aosl, h4=h4):
                        S.wait(v, "pe", peb0 + 1)
                        S.wait(v, "ssem", sb0 + 1)
                        S.inc(v.tensor_tensor(aosl, aosl, psB[0][h4:h4 + 32, :],
                                              ALU.mult), "vfree", 1)

        aodone = {"vfree": S.n["vfree"], "ssem": S.n["ssem"]}
        NTH = 1024

        fres = [qh[0][:].bitcast(F32), qh[1][:].bitcast(F32)]
        xq = [qh[2][:].bitcast(F32), qh[3][:].bitcast(F32)]
        with ExitStack() as octx:
            pg2 = [octx.enter_context(nc.psum_tensor(f"pg2{i}", [128, NTH], F32))
                   for i in range(2)]

            for half in range(2):
                hoff = NTH * half
                st_0 = [S.n["sta"], S.n["stb"]]
                ldr0 = [S.n["lda"], S.n["ldb"]]
                vf_o = S.n["vfree"]
                s3_0 = S.n["ssem"]
                pe_pre = S.n["pe"]

                def evac_o(pe0, vf0, st_0=st_0, ldr0=ldr0, s3_0=s3_0, hoff=hoff):
                    @block.scalar
                    def _(s):
                        for ot in range(8):
                            S.wait(s, "pe", pe0 + ot + 1)
                            S.wait(s, "sta" if ot % 2 == 0 else "stb",
                                   st_0[ot % 2] + 16 * (ot // 2))
                            S.inc(s.activation(out=fres[ot % 2][:, 0:NTH],
                                               in_=pg2[ot % 2][:],
                                               func=AF.Copy), "ssem", 1)

                    @block.vector
                    def _(v):
                        for ot in range(8):
                            S.wait(v, "ssem", s3_0 + ot + 1)
                            S.wait(v, "lda" if ot % 2 == 0 else "ldb",
                                   ldr0[ot % 2] + 16 * (ot // 2 + 1))
                            S.inc(v.tensor_tensor(fres[ot % 2][:, 0:NTH],
                                                  fres[ot % 2][:, 0:NTH],
                                                  xq[ot % 2][:], ALU.add), "vfree", 1)

                @block.gpsimd
                def _(g, hoff=hoff, vf_o=vf_o):
                    S.wait(g, "vfree", aodone["vfree"])
                    S.wait(g, "ssem", aodone["ssem"])
                    for ot in range(8):
                        S.inc(g.dma_start(out=xq[ot % 2][:],
                                          in_=xTb[128 * ot:128 * (ot + 1),
                                                  q0 + hoff:q0 + hoff + NTH]),
                              "lda" if ot % 2 == 0 else "ldb", 16)
                        S.wait(g, "vfree", vf_o + ot + 1)
                        S.inc(g.dma_start(out=yT[128 * ot:128 * (ot + 1),
                                                 hoff:hoff + NTH],
                                          in_=fres[ot % 2][:, 0:NTH]),
                              "sta" if ot % 2 == 0 else "stb", 16)

                emit_dense(nc, block, S, n_ot=8, nkt=4, nch=2,
                           streams=[{"hbm": wob, "transpose": False, "tiled": True}],
                           src=lambda kt, ch, hoff=hoff: ao[kt][:, hoff + 512 * ch:
                                                                hoff + 512 * (ch + 1)],
                           wb=wbo, pg=pg2, evac=evac_o,
                           tensor_pre_waits=dict(aodone) if half == 0 else
                               {"vfree": S.n["vfree"]},
                           dma_pre_gate={"pe": pe_pre} if half == 1 else None)

        @block.gpsimd
        def _(g):
            S.wait(g, "sta", S.n["sta"])
            S.wait(g, "stb", S.n["stb"])

    return nc


def build_ln_only():
    nc = bass.Bass("TRN2", target_bir_lowering=False, detect_race_conditions=False)
    xT = nc.dram_tensor("xT", [D, TLOC], F32, kind="ExternalInput").ap()
    zT = nc.dram_tensor("zT", [D, TLOC], F32, kind="ExternalOutput").ap()
    NT = 1024
    with ExitStack() as ctx:
        sb = lambda n, sh, dt: ctx.enter_context(nc.sbuf_tensor(n, sh, dt))
        xln = [sb(f"xln{i}", [128, NT], BF16) for i in range(8)]
        zf = [sb(f"zf{i}", [128, NT], F32) for i in range(8)]
        sq = [sb(f"sq{i}", [128, NT], BF16) for i in range(2)]
        xf = [sb(f"xf{i}", [128, NT], F32) for i in range(2)]
        rb = sb("rb", [128, NT], F32)
        ab = sb("ab", [128, NT], F32)
        stat = sb("stat", [1, 3 * NT], F32)
        onec = sb("onec", [128, 1], BF16)
        oner = sb("oner", [1, 128], F32)
        pg = [ctx.enter_context(nc.psum_tensor(f"pg{i}", [128, NT], F32))
              for i in range(4)]
        S = Sems(nc, ctx, ["ld", "lda", "ldb", "pe", "ssem", "vsem", "gsem",
                           "sta", "stb", "vfree", "wd0", "wd1", "scl"])
        block = ctx.enter_context(nc.Block())
        _emit_token(nc, ctx, block, xT, tensor_tok=False)

        @block.gpsimd
        def _(g):
            g.memset(onec[:], 1.0)
            g.memset(oner[:], 1.0)
            S.inc(g.nop(), "gsem", 1)

        prev = {}
        for half in range(2):
            t0 = half * NT
            bufs = dict(xln=[t[:] for t in xln], sq=sq, xf=xf, stat=stat, rb=rb,
                        ab=ab, onec=onec, oner=oner, pg=pg)
            v_norm = emit_ln(nc, block, S, bufs, xT[:, t0:t0 + NT], NT,
                             prev_gates=prev,
                             fp32_out=[zf[i][:] for i in range(8)])
            v_base = v_norm - 8
            st0 = [S.n["sta"], S.n["stb"]]

            @block.gpsimd
            def _(g, t0=t0, v_base=v_base):
                for i in range(8):
                    S.wait(g, "vsem", v_base + i + 1)
                    S.inc(g.dma_start(out=zT[128 * i:128 * (i + 1), t0:t0 + NT],
                                      in_=zf[i][:]),
                          "sta" if i % 2 == 0 else "stb", 16)
            prev = {"pe": S.n["pe"], "ssem": S.n["ssem"], "vsem": S.n["vsem"],
                    "sta": S.n["sta"], "stb": S.n["stb"]}

        @block.gpsimd
        def _(g):
            S.wait(g, "sta", S.n["sta"])
            S.wait(g, "stb", S.n["stb"])
    return nc


def build_l1mlp(final=False):
    nc = bass.Bass("TRN2", target_bir_lowering=False, detect_race_conditions=False)
    zT = nc.dram_tensor("zT", [D, TLOC + 1], F32, kind="ExternalInput").ap()
    w1p = nc.dram_tensor("w1p", [1024, 5632], BF16, kind="ExternalInput").ap()
    b1a = nc.dram_tensor("b1a", [128, 22], F32, kind="ExternalInput").ap()
    b1g = nc.dram_tensor("b1g", [128, 22], F32, kind="ExternalInput").ap()
    lng = nc.dram_tensor("lng", [128, 8], F32, kind="ExternalInput").ap()
    lnb = nc.dram_tensor("lnb", [128, 8], F32, kind="ExternalInput").ap()
    w2p = nc.dram_tensor("w2p", [2816, 1024], BF16, kind="ExternalInput").ap()
    b2c = nc.dram_tensor("b2c", [128, 8], F32, kind="ExternalInput").ap()
    if final:
        xT = nc.dram_tensor("xT", [D, TLOC], F32, kind="ExternalInput").ap()
    zoT = nc.dram_tensor("zoT", [D, TLOC], F32, kind="ExternalOutput").ap()

    NT = 1024
    with ExitStack() as ctx:
        sb = lambda n, sh, dt: ctx.enter_context(nc.sbuf_tensor(n, sh, dt))
        xln = [sb(f"xln{i}", [128, NT], BF16) for i in range(8)]
        sq = [sb(f"sq{i}", [128, NT], BF16) for i in range(2)]
        xf = [sb(f"xf{i}", [128, NT], F32) for i in range(2)]
        rb = sb("rb", [128, NT], F32)
        ab = sb("ab", [128, NT], F32)
        stat = sb("stat", [1, 3 * NT], F32)
        onec = sb("onec", [128, 1], BF16)
        oner = sb("oner", [1, 128], F32)
        h2 = [sb(f"h2_{i}", [128, NT], BF16) for i in range(22)]
        wbs = [sb(f"wbs{i}", [128, 1024], BF16) for i in range(4)]
        wb2 = [sb(f"wb2{i}", [128, 2816], BF16) for i in range(2)]
        tgel = [sb(f"tgel{i}", [128, NT], BF16) for i in range(2)]
        fres = [sb(f"fres{i}", [128, NT], F32) for i in range(2)]
        xrf = [sb(f"xrf{i}", [128, NT], F32) for i in range(2)] if final else None
        cols = {}
        for nm, hb, w_ in [("b1a", b1a, 22), ("b1g", b1g, 22), ("lng", lng, 8),
                           ("lnb", lnb, 8), ("b2c", b2c, 8)]:
            cols[nm] = sb("c_" + nm, [128, w_], F32)
        pg = [ctx.enter_context(nc.psum_tensor(f"pg{i}", [128, NT], F32))
              for i in range(4)]
        S = Sems(nc, ctx, ["ld", "lda", "ldb", "pe", "ssem", "vsem", "wd0", "wd1",
                           "gsem", "sta", "stb", "scl", "vfree"])
        block = ctx.enter_context(nc.Block())

        @block.gpsimd
        def _(g):
            g.memset(onec[:], 1.0)
            g.memset(oner[:], 1.0)
            S.inc(g.nop(), "gsem", 1)

        @block.sync
        def _(sy):
            for nm in cols:
                S.inc(sy.dma_start(out=cols[nm][:],
                                   in_={"b1a": b1a, "b1g": b1g, "lng": lng,
                                        "lnb": lnb, "b2c": b2c}[nm]), "scl", 16)
        scl_done = S.n["scl"]

        prev = {}
        for half in range(2):
            t0 = half * NT
            xin = [zT[128 * i:128 * (i + 1), 1 + t0:1 + t0 + NT] for i in range(4)] + \
                  [zT[128 * i:128 * (i + 1), t0:t0 + NT] for i in range(4, 8)]
            bufs = dict(xln=[t[:] for t in xln], sq=sq, xf=xf, stat=stat, rb=rb,
                        ab=ab, onec=onec, oner=oner, pg=pg)
            v_norm = emit_ln(nc, block, S, bufs, xin, NT, prev_gates=prev)

            s_g0 = S.n["ssem"]

            @block.scalar
            def _(s):
                S.wait(s, "scl", scl_done)
                S.wait(s, "vsem", v_norm)
                for i in range(8):
                    S.inc(s.activation(out=xln[i][:], in_=xln[i][:],
                                       func=AF.Identity,
                                       scale=cols["lng"][:, i:i + 1],
                                       bias=cols["lnb"][:, i:i + 1]), "ssem", 1)
            s_gain = S.n["ssem"]

            wd_0 = [S.n["wd0"], S.n["wd1"]]
            pe_0 = S.n["pe"]
            se_0 = S.n["ssem"]
            vf_0 = S.n["vfree"]

            @block.sync
            def _(sy, pe_0=pe_0, wd_0=wd_0):
                for p_ in range(22):
                    if p_ >= 2:
                        S.wait(sy, "pe", pe_0 + 2 * (p_ - 1))
                    wsem = "wd0" if p_ % 2 == 0 else "wd1"
                    for kt in range(8):
                        S.inc(sy.dma_start(
                            out=wbs[2 * (p_ % 2)][:, 128 * kt:128 * (kt + 1)],
                            in_=w1p[128 * kt:128 * (kt + 1),
                                    128 * p_:128 * (p_ + 1)]), wsem, 16)
                        S.inc(sy.dma_start(
                            out=wbs[2 * (p_ % 2) + 1][:, 128 * kt:128 * (kt + 1)],
                            in_=w1p[128 * kt:128 * (kt + 1),
                                    2816 + 128 * p_:2816 + 128 * (p_ + 1)]), wsem, 16)

            @block.tensor
            def _(t, pe_0=pe_0, se_0=se_0, vf_0=vf_0, wd_0=wd_0, s_gain=s_gain):
                S.wait(t, "ssem", s_gain)
                for p_ in range(22):
                    S.wait(t, "wd0" if p_ % 2 == 0 else "wd1",
                           wd_0[p_ % 2] + 256 * (p_ // 2 + 1))
                    if p_ >= 2:
                        S.wait(t, "vfree", vf_0 + (p_ - 1))
                    g2 = p_ % 2
                    for w_i in range(2):
                        psum = pg[2 * w_i + g2]
                        buf = wbs[2 * g2 + w_i]
                        last = None
                        for kt in range(8):
                            for ch in range(2):
                                sl = slice(512 * ch, 512 * (ch + 1))
                                last = t.matmul(psum[:, sl],
                                                buf[:, 128 * kt:128 * (kt + 1)],
                                                xln[kt][:, sl], start=(kt == 0),
                                                stop=(kt == 7),
                                                skip_group_check=True)
                        S.inc(last, "pe", 1)

            @block.scalar
            def _(s, pe_0=pe_0, vf_0=vf_0, se_0=se_0):
                for p_ in range(22):
                    g2 = p_ % 2
                    S.wait(s, "pe", pe_0 + 2 * p_ + 2)
                    if p_ >= 2:
                        S.wait(s, "vfree", vf_0 + (p_ - 1))
                    S.inc(s.activation(out=tgel[g2][:], in_=pg[2 + g2][:, 0:NT],
                                       func=AF.Gelu,
                                       bias=cols["b1g"][:, p_:p_ + 1]), "ssem", 1)

            @block.vector
            def _(v, pe_0=pe_0, se_0=se_0):
                for p_ in range(22):
                    g2 = p_ % 2
                    S.wait(v, "pe", pe_0 + 2 * p_ + 1)
                    S.wait(v, "ssem", se_0 + p_ + 1)
                    S.inc(v.scalar_tensor_tensor(h2[p_][:], pg[g2][:, 0:NT],
                                                 cols["b1a"][:, p_:p_ + 1],
                                                 tgel[g2][:], ALU.add, ALU.mult),
                          "vfree", 1)
            vf_w1 = S.n["vfree"]

            st_0 = [S.n["sta"], S.n["stb"]]
            ldr0 = [S.n["lda"], S.n["ldb"]]
            s3_0 = S.n["ssem"]
            v3_0 = S.n["vsem"]
            vfo = S.n["vfree"]
            wd2_0 = [S.n["wd0"], S.n["wd1"]]
            pe2_0 = S.n["pe"]

            @block.sync
            def _(sy, pe2_0=pe2_0, wd2_0=wd2_0, vf_w1=vf_w1):
                S.wait(sy, "vfree", vf_w1)
                for ot in range(8):
                    if ot >= 2:
                        S.wait(sy, "pe", pe2_0 + (ot - 1))
                    wsem = "wd0" if ot % 2 == 0 else "wd1"
                    for kt in range(22):
                        S.inc(sy.dma_start(
                            out=wb2[ot % 2][:, 128 * kt:128 * (kt + 1)],
                            in_=w2p[128 * kt:128 * (kt + 1),
                                    128 * ot:128 * (ot + 1)]), wsem, 16)

            @block.tensor
            def _(t, pe2_0=pe2_0, wd2_0=wd2_0, vf_w1=vf_w1, vfo=vfo):
                S.wait(t, "vfree", vf_w1)
                for ot in range(8):
                    S.wait(t, "wd0" if ot % 2 == 0 else "wd1",
                           wd2_0[ot % 2] + 16 * 22 * (ot // 2 + 1))
                    if ot >= 2:
                        S.wait(t, "vfree", vfo + (ot - 1))
                    last = None
                    for kt in range(22):
                        for ch in range(2):
                            sl = slice(512 * ch, 512 * (ch + 1))
                            last = t.matmul(pg[ot % 2][:, sl],
                                            wb2[ot % 2][:, 128 * kt:128 * (kt + 1)],
                                            h2[kt][:, sl], start=(kt == 0),
                                            stop=(kt == 21), skip_group_check=True)
                    S.inc(last, "pe", 1)

            @block.scalar
            def _(s, pe2_0=pe2_0, st_0=st_0):
                for ot in range(8):
                    S.wait(s, "pe", pe2_0 + ot + 1)
                    S.wait(s, "sta" if ot % 2 == 0 else "stb",
                           st_0[ot % 2] + 16 * (ot // 2))
                    S.inc(s.activation(out=fres[ot % 2][:], in_=pg[ot % 2][:, 0:NT],
                                       func=AF.Identity,
                                       bias=cols["b2c"][:, ot:ot + 1]), "ssem", 1)

            @block.vector
            def _(v, s3_0=s3_0, ldr0=ldr0):
                for ot in range(8):
                    S.wait(v, "ssem", s3_0 + ot + 1)
                    nload = 2 if final else 1
                    S.wait(v, "lda" if ot % 2 == 0 else "ldb",
                           ldr0[ot % 2] + 16 * nload * (ot // 2 + 1))
                    if final:
                        v.tensor_tensor(fres[ot % 2][:], fres[ot % 2][:],
                                        xrf[ot % 2][:], ALU.add)
                    S.inc(v.tensor_tensor(fres[ot % 2][:], fres[ot % 2][:],
                                          xf[ot % 2][:], ALU.add), "vfree", 1)

            @block.gpsimd
            def _(g, t0=t0, vfo=vfo, ldr0=ldr0):
                S.wait(g, "vsem", v_norm)
                for ot in range(8):
                    par = "lda" if ot % 2 == 0 else "ldb"
                    S.inc(g.dma_start(out=xf[ot % 2][:],
                                      in_=zT[128 * ot:128 * (ot + 1),
                                             1 + t0:1 + t0 + NT]), par, 16)
                    if final:
                        S.inc(g.dma_start(out=xrf[ot % 2][:],
                                          in_=xT[128 * ot:128 * (ot + 1),
                                                 t0:t0 + NT]), par, 16)
                    S.wait(g, "vfree", vfo + ot + 1)
                    S.inc(g.dma_start(out=zoT[128 * ot:128 * (ot + 1), t0:t0 + NT],
                                      in_=fres[ot % 2][:]),
                          "sta" if ot % 2 == 0 else "stb", 16)

            prev = {"pe": S.n["pe"], "ssem": S.n["ssem"], "vsem": S.n["vsem"],
                    "vfree": S.n["vfree"], "sta": S.n["sta"], "stb": S.n["stb"]}

        @block.gpsimd
        def _(g):
            S.wait(g, "sta", S.n["sta"])
            S.wait(g, "stb", S.n["stb"])

    return nc


def build_l1attn():
    nc = bass.Bass("TRN2", target_bir_lowering=False, detect_race_conditions=False)
    zT = nc.dram_tensor("zT", [D, 2688], F32, kind="ExternalInput").ap()
    wqkvl = nc.dram_tensor("wqkvl", [1024, 1536], BF16, kind="ExternalInput").ap()
    wol = nc.dram_tensor("wol", [512, 1024], BF16, kind="ExternalInput").ap()
    qscol = nc.dram_tensor("qscol", [128, 1], F32, kind="ExternalInput").ap()
    kscol = nc.dram_tensor("kscol", [128, 1], F32, kind="ExternalInput").ap()
    qsrow = nc.dram_tensor("qsrow", [1, 64], F32, kind="ExternalInput").ap()
    ksrow = nc.dram_tensor("ksrow", [1, 64], F32, kind="ExternalInput").ap()
    csqh = nc.dram_tensor("csqh", [128, 2048], BF16, kind="ExternalInput").ap()
    ssqh = nc.dram_tensor("ssqh", [128, 2048], BF16, kind="ExternalInput").ap()
    cskh = nc.dram_tensor("cskh", [128, 2560], BF16, kind="ExternalInput").ap()
    sskh = nc.dram_tensor("sskh", [128, 2560], BF16, kind="ExternalInput").ap()
    p32h = nc.dram_tensor("p32h", [128, 128], BF16, kind="ExternalInput").ap()
    bselhh = nc.dram_tensor("bselhh", [128, 2], BF16, kind="ExternalInput").ap()
    rephh = nc.dram_tensor("rephh", [2, 128], F32, kind="ExternalInput").ap()
    maskp = nc.dram_tensor("maskp", [4, 128, 1024], BF16, kind="ExternalInput").ap()
    killc = nc.dram_tensor("killc", [128, 1], F32, kind="ExternalInput").ap()
    zoT = nc.dram_tensor("zoT", [D, TLOC], F32, kind="ExternalOutput").ap()

    KV = 2560
    with ExitStack() as ctx:
        sb = lambda n, sh, dt: ctx.enter_context(nc.sbuf_tensor(n, sh, dt))
        zb = [sb(f"zb{i}", [128, 2688], BF16) for i in range(8)]
        hshu = [sb(f"hshu{i}", [128, 2688], BF16) for i in range(4)]
        ao = [hshu[i][:, 0:TLOC] for i in range(4)]
        vvt = [sb(f"vvt{i}", [128, 520], BF16) for i in range(20)]
        qr = [sb(f"qr{i}", [128, TLOC], BF16) for i in range(4)]
        kr = [sb(f"kr{i}", [128, KV], BF16) for i in range(4)]
        wvs = sb("wvs", [128, 4096], BF16)
        wqs = [sb(f"wqs{i}", [128, 1024], BF16) for i in range(2)]
        wos = [sb(f"wos{i}", [128, 512], BF16) for i in range(2)]
        tq = [sb(f"tq{i}", [128, 512], BF16) for i in range(2)]
        pt = [sb(f"pt{i}", [128, 1024], BF16) for i in range(2)]
        csq = sb("csq", [128, 2048], BF16)
        ssq = sb("ssq", [128, 2048], BF16)
        csk = sb("csk", [128, 2560], BF16)
        ssk = sb("ssk", [128, 2560], BF16)
        p32 = sb("p32", [128, 128], BF16)
        bselh = sb("bselh", [128, 2], BF16)
        reph = sb("reph", [2, 128], F32)
        qsc = sb("qsc", [128, 1], F32)
        ksc = sb("ksc", [128, 1], F32)
        kil = sb("kil", [128, 1], F32)
        maskt = [sb(f"maskt{i}", [128, 1024], BF16) for i in range(4)]
        grow = sb("grow", [1, 128], F32)
        mtmp = sb("mtmp", [1, 4], F32)
        negm = sb("negm", [128, 1], F32)
        rrow = sb("rrow", [1, 512], F32)
        rq2 = sb("rq2", [2, 2560], F32)
        oner = sb("oner", [1, 128], F32)
        fres = [sb(f"fres{i}", [128, 1024], F32) for i in range(2)]
        xq = [sb(f"xq{i}", [128, 1024], F32) for i in range(2)]
        S = Sems(nc, ctx, ["ld", "lda", "ldb", "pe", "pes", "pvs", "ssem", "sact",
                           "vsem", "wd0", "wd1", "gsem", "sta", "stb", "scl",
                           "vfree"])
        block = ctx.enter_context(nc.Block())

        @block.gpsimd
        def _(g):
            g.memset(oner[:], 1.0)
            for tt in range(20):
                g.memset(vvt[tt][:], 1.0)
            S.inc(g.nop(), "gsem", 1)

        @block.sync
        def _(sy):
            for dst, src in [(csq, csqh), (ssq, ssqh), (csk, cskh), (ssk, sskh),
                             (p32, p32h), (bselh, bselhh), (reph, rephh),
                             (qsc, qscol), (ksc, kscol), (kil, killc)]:
                S.inc(sy.dma_start(out=dst[:], in_=src), "scl", 16)
            for mi in range(4):
                S.inc(sy.dma_start(out=maskt[mi][:], in_=maskp[mi]), "scl", 16)
            S.inc(sy.dma_start(out=grow[:, 0:64], in_=qsrow), "scl", 16)
            S.inc(sy.dma_start(out=grow[:, 64:128], in_=ksrow), "scl", 16)
            for ct in range(8):
                S.inc(sy.dma_start(out=wvs[:, 512 * ct:512 * (ct + 1)],
                                   in_=wqkvl[128 * ct:128 * (ct + 1), 1024:1536]),
                      "scl", 16)
        scl_done = S.n["scl"]

        ld0 = S.n["ld"]

        @block.gpsimd
        def _(g):
            for i in range(8):
                S.inc(g.dma_start(out=zb[i][:], in_=zT[128 * i:128 * (i + 1), :]),
                      "ld", 16)
            for j in range(4):
                S.inc(g.dma_start(out=hshu[j][:, 1:2688],
                                  in_=zT[512 + 128 * j:512 + 128 * (j + 1), 0:2687]),
                      "ld", 16)
        ld_in = S.n["ld"]
        hsh = [zb[0], zb[1], zb[2], zb[3]] + hshu

        with ExitStack() as pctx:
            pA = pctx.enter_context(nc.psum_tensor("pA", [128, 1024], F32))
            pB = pctx.enter_context(nc.psum_tensor("pB", [128, 1024], F32))
            pMSp = [pctx.enter_context(nc.psum_tensor(f"pMSp{i}", [2, 512], F32))
                    for i in range(2)]
            pAh = [pA[:, 0:512], pA[:, 512:1024]]
            pBh = [pB[:, 0:512], pB[:, 512:1024]]

            vm0 = S.n["vsem"]

            @block.vector
            def _(v):
                S.wait(v, "scl", scl_done)
                v.tensor_tensor(grow[:, 0:64], grow[:, 0:64], grow[:, 64:128],
                                ALU.mult)
                v.tensor_reduce(mtmp[:, 0:1], grow[:, 0:64], AX.X, ALU.max,
                                apply_absolute_value=True)
                S.inc(v.tensor_scalar(mtmp[:, 1:2], mtmp[:, 0:1], -8.0, None,
                                      ALU.mult), "vsem", 1)
            vm1 = S.n["vsem"]

            @block.tensor
            def _(t):
                S.wait(t, "vsem", vm1)
                S.inc(t.matmul(pA[:, 0:1], oner[:], mtmp[:, 1:2], start=True,
                               stop=True, skip_group_check=True), "pe", 1)
            pe_m = S.n["pe"]

            @block.scalar
            def _(s):
                S.wait(s, "pe", pe_m)
                S.inc(s.activation(out=negm[:], in_=pA[:, 0:1], func=AF.Copy),
                      "gsem", 1)
            g_m = S.n["gsem"]

            pe_v0 = S.n["pe"]
            se_v0 = S.n["ssem"]

            @block.tensor
            def _(t):
                S.wait(t, "ld", ld_in)
                S.wait(t, "scl", scl_done)
                S.wait(t, "gsem", g_m)
                for tt in range(20):
                    if tt >= 2:
                        S.wait(t, "ssem", se_v0 + tt - 1)
                    sl = slice(128 + 128 * tt, 256 + 128 * tt)
                    last = None
                    for ct in range(8):
                        last = t.matmul(pAh[tt % 2], hsh[ct][:, sl],
                                        wvs[:, 512 * ct:512 * (ct + 1)],
                                        start=(ct == 0), stop=(ct == 7),
                                        skip_group_check=True)
                    S.inc(last, "pe", 1)

            @block.scalar
            def _(s):
                for tt in range(20):
                    S.wait(s, "pe", pe_v0 + tt + 1)
                    S.inc(s.activation(
                        out=vvt[tt][:, 0:520].rearrange("p (h e) -> p h e",
                                                        h=8)[:, :, 0:64],
                        in_=pAh[tt % 2].rearrange("p (h d) -> p h d", h=8),
                        func=AF.Copy), "ssem", 1)
            v_done = S.n["ssem"]

            def emit_raw(col0, n_ch, base_col, dst):
                wd_0 = [S.n["wd0"], S.n["wd1"]]
                pe_0 = S.n["pe"]
                se_0 = S.n["ssem"]

                @block.sync
                def _(sy, wd_0=wd_0, pe_0=pe_0):
                    S.wait(sy, "pe", pe_0)
                    for mo in range(4):
                        if mo >= 2:
                            S.wait(sy, "pe", pe_0 + n_ch * (mo - 1))
                        wsem = "wd0" if mo % 2 == 0 else "wd1"
                        for kt in range(8):
                            S.inc(sy.dma_start(
                                out=wqs[mo % 2][:, 128 * kt:128 * (kt + 1)],
                                in_=wqkvl[128 * kt:128 * (kt + 1),
                                          col0 + 128 * mo:col0 + 128 * (mo + 1)]),
                                wsem, 16)

                @block.tensor
                def _(t, wd_0=wd_0, pe_0=pe_0, se_0=se_0):
                    S.wait(t, "ssem", se_0)
                    for j in range(4 * n_ch):
                        mo, ch = j // n_ch, j % n_ch
                        if ch == 0:
                            S.wait(t, "wd0" if mo % 2 == 0 else "wd1",
                                   wd_0[mo % 2] + 128 * (mo // 2 + 1))
                        if j >= 2:
                            S.wait(t, "ssem", se_0 + j - 1)
                        sl = slice(base_col + 512 * ch, base_col + 512 * (ch + 1))
                        last = None
                        for kt in range(8):
                            last = t.matmul(pAh[j % 2],
                                            wqs[mo % 2][:, 128 * kt:128 * (kt + 1)],
                                            hsh[kt][:, sl], start=(kt == 0),
                                            stop=(kt == 7), skip_group_check=True)
                        S.inc(last, "pe", 1)

                @block.scalar
                def _(s, pe_0=pe_0):
                    for j in range(4 * n_ch):
                        mo, ch = j // n_ch, j % n_ch
                        S.wait(s, "pe", pe_0 + j + 1)
                        S.inc(s.activation(out=dst[mo][:, 512 * ch:512 * (ch + 1)],
                                           in_=pAh[j % 2], func=AF.Copy),
                              "ssem", 1)

            emit_raw(0, 4, 640, qr)
            se_q = S.n["ssem"]
            emit_raw(512, 5, 128, kr)
            se_k = S.n["ssem"]

            def emit_nr(dst, n_ch, cs, ss, sccol, se_gate):
                for mo in range(4):
                    sa_0 = S.n["sact"]
                    pes_0 = S.n["pes"]
                    pvs_0 = S.n["pvs"]
                    pe_0 = S.n["pe"]
                    vs_0 = S.n["vsem"]

                    @block.scalar
                    def _(s, mo=mo, sa_0=sa_0, pes_0=pes_0, se_gate=se_gate,
                          vs_0=vs_0):
                        if mo == 0:
                            S.wait(s, "ssem", se_gate)
                        for c in range(n_ch):
                            if c >= 2:
                                S.wait(s, "vsem", vs_0 + 3 * (c - 2) + 3)
                            else:
                                S.wait(s, "vsem", vs_0)
                            S.inc(s.activation(out=tq[c % 2][:, 0:512],
                                               in_=dst[mo][:, 512 * c:512 * (c + 1)],
                                               func=AF.Square), "sact", 1)
                            S.wait(s, "pes", pes_0 + c + 1)
                            S.inc(s.activation(out=rq2[:, 512 * c:512 * (c + 1)],
                                               in_=pMSp[c % 2][:], func=AF.Sqrt),
                                  "sact", 1)

                    @block.tensor
                    def _(t, mo=mo, sa_0=sa_0, pes_0=pes_0, pvs_0=pvs_0,
                          pe_0=pe_0, vs_0=vs_0):
                        for c in range(n_ch + 1):
                            if c < n_ch:
                                S.wait(t, "sact", sa_0 + 2 * c + 1)
                                S.inc(t.matmul(pMSp[c % 2][:], bselh[:],
                                               tq[c % 2][:, 0:512], start=True,
                                               stop=True, skip_group_check=True),
                                      "pes", 1)
                            if c >= 1:
                                cc = c - 1
                                S.wait(t, "vsem", vs_0 + 3 * cc + 1)
                                S.inc(t.matmul(pBh[cc % 2], reph[:],
                                               rq2[:, 512 * cc:512 * (cc + 1)],
                                               start=True, stop=True,
                                               skip_group_check=True), "pvs", 1)
                                S.wait(t, "vsem", vs_0 + 3 * cc + 2)
                                S.inc(t.matmul(pAh[cc % 2], p32[:],
                                               dst[mo][:, 512 * cc:512 * (cc + 1)],
                                               start=True, stop=True,
                                               skip_group_check=True), "pe", 1)

                    @block.vector
                    def _(v, mo=mo, sa_0=sa_0, pvs_0=pvs_0, pe_0=pe_0, vs_0=vs_0):
                        for c in range(n_ch):
                            sl = slice(512 * c, 512 * (c + 1))
                            dsl = dst[mo][:, sl]
                            rsl = rq2[:, sl]
                            S.wait(v, "sact", sa_0 + 2 * c + 2)
                            v.tensor_scalar(rsl, rsl, 1e-12, None, ALU.max)
                            S.inc(v.reciprocal(rsl, rsl), "vsem", 1)
                            S.wait(v, "pvs", pvs_0 + c + 1)
                            S.inc(v.scalar_tensor_tensor(dsl, dsl, sccol[:, 0:1],
                                                         pBh[c % 2], ALU.mult,
                                                         ALU.mult), "vsem", 1)
                            v.tensor_tensor(pt[c % 2][:, 0:512], dsl, cs[:, sl],
                                            ALU.mult)
                            S.wait(v, "pe", pe_0 + c + 1)
                            v.tensor_tensor(tq[c % 2][:, 0:512], pAh[c % 2],
                                            ss[:, sl], ALU.mult)
                            S.inc(v.tensor_tensor(dsl, pt[c % 2][:, 0:512],
                                                  tq[c % 2][:, 0:512], ALU.add),
                                  "vsem", 1)

            emit_nr(qr, 4, csq, ssq, qsc, se_q)
            emit_nr(kr, 5, csk, ssk, ksc, se_k)
            nr_done = {"vsem": S.n["vsem"], "pe": S.n["pe"], "pvs": S.n["pvs"]}

        with ExitStack() as actx:
            psS = [actx.enter_context(nc.psum_tensor(f"psS{i}", [128, 1024], F32))
                   for i in range(2)]
            psO = actx.enter_context(nc.psum_tensor("psO", [65, 512], F32))
            psB = actx.enter_context(nc.psum_tensor("psB", [128, 512], F32))

            first_att = True
            for h in range(8):
                for w in range(4):
                    h2 = 64 * (h % 2)
                    qsl = slice(512 * w, 512 * (w + 1))
                    pes0 = S.n["pes"]
                    pvs0 = S.n["pvs"]
                    se0 = S.n["ssem"]
                    vs_prev = S.n["vsem"]
                    vmsk0 = S.n["vsem"]

                    @block.tensor
                    def _(t, h=h, w=w, h2=h2, qsl=qsl, pes0=pes0, pvs0=pvs0,
                          se0=se0, vs_prev=vs_prev, vmsk0=vmsk0,
                          first_att=first_att):
                        if first_att:
                            for nm, v_ in nr_done.items():
                                S.wait(t, nm, v_)
                            S.wait(t, "ssem", v_done)
                        S.wait(t, "vsem", vs_prev)
                        order = []
                        for kt in range(8):
                            order.append(("s", kt))
                            if kt >= 2:
                                order.append(("p", kt - 2))
                        order += [("p", 6), ("p", 7)]
                        for kind, kt in order:
                            kb = 512 * w + 128 * kt
                            if kind == "s":
                                if kt >= 4:
                                    S.wait(t, "ssem", se0 + (kt // 2 - 1))
                                S.inc(t.matmul(
                                    psS[(kt // 2) % 2][:, 512 * (kt % 2):
                                                       512 * (kt % 2) + 512],
                                    kr[h // 2][h2:h2 + 64, kb:kb + 128],
                                    qr[h // 2][h2:h2 + 64, qsl],
                                    start=True, stop=True,
                                    skip_group_check=True), "pes", 1)
                            else:
                                S.wait(t, "vsem", vmsk0 + (kt // 2 + 1))
                                S.inc(t.matmul(psO[:],
                                               vvt[4 * w + kt][:, 65 * h:65 * h + 65],
                                               pt[(kt // 2) % 2][:, 512 * (kt % 2):
                                                                 512 * (kt % 2) + 512],
                                               start=(kt == 0), stop=(kt == 7),
                                               skip_group_check=True), "pvs", 1)

                    @block.scalar
                    def _(s, pes0=pes0, pvs0=pvs0, first_att=first_att):
                        if first_att:
                            S.wait(s, "gsem", g_m)
                        for kp in range(4):
                            S.wait(s, "pes", pes0 + 2 * kp + 2)
                            if kp >= 2:
                                S.wait(s, "pvs", pvs0 + 2 * (kp - 1))
                            S.inc(s.activation(out=pt[kp % 2][:], in_=psS[kp % 2][:],
                                               func=AF.Exp, scale=8.0,
                                               bias=negm[:, 0:1]), "ssem", 1)

                    @block.vector
                    def _(v, se0=se0, vmsk0=vmsk0, w=w, pvs0=pvs0):
                        for kp in range(4):
                            S.wait(v, "ssem", se0 + kp + 1)
                            if kp >= 2:
                                S.wait(v, "pvs", pvs0 + 2 * (kp - 1))
                            v.tensor_tensor(pt[kp % 2][:], pt[kp % 2][:],
                                            maskt[kp][:], ALU.mult)
                            if w == 0 and kp < 2:
                                S.inc(v.tensor_scalar(pt[kp % 2][:], pt[kp % 2][:],
                                                      kil[:, 0:1], None, ALU.mult),
                                      "vsem", 1)
                            else:
                                S.inc(v.nop(), "vsem", 1)

                    pvs1 = S.n["pvs"]
                    sb0 = S.n["ssem"]
                    aosl = ao[h // 2][h2:h2 + 64, qsl]

                    @block.scalar
                    def _(s, pvs1=pvs1, aosl=aosl):
                        S.wait(s, "pvs", pvs1)
                        S.inc(s.activation(out=aosl, in_=psO[0:64, :],
                                           func=AF.Copy), "ssem", 1)

                    @block.vector
                    def _(v, pvs1=pvs1):
                        S.wait(v, "pvs", pvs1)
                        S.inc(v.reciprocal(rrow[:], psO[64:65, :]), "vsem", 1)
                    vb1 = S.n["vsem"]
                    peb0 = S.n["pe"]
                    vf_prev = S.n["vfree"]

                    @block.tensor
                    def _(t, vb1=vb1, vf_prev=vf_prev):
                        S.wait(t, "vsem", vb1)
                        S.wait(t, "vfree", vf_prev)
                        S.inc(t.matmul(psB[:], oner[:], rrow[:], start=True,
                                       stop=True, skip_group_check=True), "pe", 1)

                    @block.vector
                    def _(v, peb0=peb0, sb0=sb0, aosl=aosl, h2=h2):
                        S.wait(v, "pe", peb0 + 1)
                        S.wait(v, "ssem", sb0 + 1)
                        S.inc(v.tensor_tensor(aosl, aosl, psB[h2:h2 + 64, :],
                                              ALU.mult), "vfree", 1)
                    first_att = False

        aodone = {"vfree": S.n["vfree"], "ssem": S.n["ssem"]}
        NTH = 1024
        with ExitStack() as octx:
            pg2 = [octx.enter_context(nc.psum_tensor(f"pg2{i}", [128, NTH], F32))
                   for i in range(2)]

            for half in range(2):
                hoff = NTH * half
                st_0 = [S.n["sta"], S.n["stb"]]
                ldr0 = [S.n["lda"], S.n["ldb"]]
                vf_o = S.n["vfree"]
                s3_0 = S.n["ssem"]
                pe_pre = S.n["pe"]

                def evac_o(pe0, vf0, st_0=st_0, ldr0=ldr0, s3_0=s3_0, hoff=hoff):
                    @block.scalar
                    def _(s):
                        for ot in range(8):
                            S.wait(s, "pe", pe0 + ot + 1)
                            S.wait(s, "sta" if ot % 2 == 0 else "stb",
                                   st_0[ot % 2] + 16 * (ot // 2))
                            S.inc(s.activation(out=fres[ot % 2][:],
                                               in_=pg2[ot % 2][:],
                                               func=AF.Copy), "ssem", 1)

                    @block.vector
                    def _(v):
                        for ot in range(8):
                            S.wait(v, "ssem", s3_0 + ot + 1)
                            S.wait(v, "lda" if ot % 2 == 0 else "ldb",
                                   ldr0[ot % 2] + 16 * (ot // 2 + 1))
                            S.inc(v.tensor_tensor(fres[ot % 2][:], fres[ot % 2][:],
                                                  xq[ot % 2][:], ALU.add),
                                  "vfree", 1)

                @block.gpsimd
                def _(g, hoff=hoff, vf_o=vf_o):
                    S.wait(g, "vfree", aodone["vfree"])
                    S.wait(g, "ssem", aodone["ssem"])
                    for ot in range(8):
                        S.inc(g.dma_start(out=xq[ot % 2][:],
                                          in_=zT[128 * ot:128 * (ot + 1),
                                                 640 + hoff:640 + hoff + NTH]),
                              "lda" if ot % 2 == 0 else "ldb", 16)
                        S.wait(g, "vfree", vf_o + ot + 1)
                        S.inc(g.dma_start(out=zoT[128 * ot:128 * (ot + 1),
                                                  hoff:hoff + NTH],
                                          in_=fres[ot % 2][:]),
                              "sta" if ot % 2 == 0 else "stb", 16)

                emit_dense(nc, block, S, n_ot=8, nkt=4, nch=2,
                           streams=[{"hbm": wol, "transpose": False, "tiled": True}],
                           src=lambda kt, ch, hoff=hoff: ao[kt][:, hoff + 512 * ch:
                                                                hoff + 512 * (ch + 1)],
                           wb=wos, pg=pg2, evac=evac_o,
                           tensor_pre_waits=dict(aodone) if half == 0 else
                               {"vfree": S.n["vfree"]},
                           dma_pre_gate={"pe": pe_pre} if half == 1 else
                               {"pes": S.n["pes"]})

        @block.gpsimd
        def _(g):
            S.wait(g, "sta", S.n["sta"])
            S.wait(g, "stb", S.n["stb"])

    return nc


def l0attnb_consts():
    import ml_dtypes
    bf = lambda a: a.astype(ml_dtypes.bfloat16)
    off = [0, 8, 24, 48, 80, 120, 168, 224]
    cgarg = np.zeros((2, 128, 448), np.float32)
    cseln = np.zeros((2, 128, 448), np.float32)
    for par in range(2):
        R = 64 * par
        for I in range(8):
            I0 = 8 * I
            for s in range(8 * (I + 1)):
                c = off[I] + s
                if s < I0:
                    cgarg[par, R + s + 1:R + I0 + 1, c] = 1.0
                elif s > I0:
                    cgarg[par, R + I0 + 1:R + s + 1, c] = -1.0
                cseln[par, R + s, c] = 1.0
        for s in range(64):
            c = 384 + s
            cgarg[par, R + s + 1:R + 64, c] = 1.0
            cseln[par, R + s, c] = 1.0
    cselp = np.zeros((64, 288), np.float32)
    ccm = np.zeros((3, 128, 64), np.float32)
    for I in range(8):
        for s in range(8 * (I + 1)):
            c = off[I] + s
            cselp[s, c] = 1.0
            ccm[c // 128, c % 128, 8 * I:8 * I + 8] = 1.0
    ctm = np.triu(np.ones((64, 64), np.float32))
    csc64 = np.ones((128, 2048), np.float32); csc64[:, ::64] = 0.0
    csc8 = np.ones((128, 2048), np.float32); csc8[:, ::8] = 0.0
    sel63 = np.zeros((2, 128, 128), np.float32)
    for par in range(2):
        sel63[par, 64 * par + 63, :] = 1.0
    return dict(cgargh=bf(cgarg), cselnh=bf(cseln), cselph=bf(cselp),
                ccmh=bf(ccm), ctmh=bf(ctm), csc64h=bf(csc64), csc8h=bf(csc8),
                sel63h=bf(sel63))


def build_l0attnb():
    nc = bass.Bass("TRN2", target_bir_lowering=False, detect_race_conditions=False)
    TH = 2048
    zTb = nc.dram_tensor("zTb", [D, T], F32, kind="ExternalInput").ap()
    wqT = nc.dram_tensor("wqT", [1024, 512], BF16, kind="ExternalInput").ap()
    wkT = nc.dram_tensor("wkT", [1024, 512], BF16, kind="ExternalInput").ap()
    wvT = nc.dram_tensor("wvT", [1024, 512], BF16, kind="ExternalInput").ap()
    woT = nc.dram_tensor("woT", [512, 1024], BF16, kind="ExternalInput").ap()
    gc_h = {nm: nc.dram_tensor(nm, [128, 4], F32, kind="ExternalInput").ap()
            for nm in ["wgc", "bgc", "wlc", "blc"]}
    cgargh = nc.dram_tensor("cgargh", [2, 128, 448], BF16, kind="ExternalInput").ap()
    cselnh = nc.dram_tensor("cselnh", [2, 128, 448], BF16, kind="ExternalInput").ap()
    cselph = nc.dram_tensor("cselph", [64, 288], BF16, kind="ExternalInput").ap()
    ccmh = nc.dram_tensor("ccmh", [3, 128, 64], BF16, kind="ExternalInput").ap()
    ctmh = nc.dram_tensor("ctmh", [64, 64], BF16, kind="ExternalInput").ap()
    csc64h = nc.dram_tensor("csc64h", [128, 2048], BF16, kind="ExternalInput").ap()
    csc8h = nc.dram_tensor("csc8h", [128, 2048], BF16, kind="ExternalInput").ap()
    sel63h = nc.dram_tensor("sel63h", [2, 128, 128], BF16, kind="ExternalInput").ap()
    poT = nc.dram_tensor("poT", [D, T], F32, kind="ExternalOutput").ap()

    rhoB = nc.dram_tensor("rhoB", [512, T], BF16).ap()
    nuB = nc.dram_tensor("nuB", [512, T], BF16).ap()
    e1B = nc.dram_tensor("e1B", [512, T], BF16).ap()
    efB = nc.dram_tensor("efB", [512, T], BF16).ap()
    vB = nc.dram_tensor("vB", [512, T], BF16).ap()
    yB = nc.dram_tensor("yB", [T, 512], BF16).ap()

    with ExitStack() as ctx:
        sb = lambda n, sh, dt: ctx.enter_context(nc.sbuf_tensor(n, sh, dt)).ap()
        xln = [sb(f"xln{i}", [128, TH], BF16) for i in range(8)]
        qT = [sb(f"qT{i}", [128, TH], BF16) for i in range(4)]
        kT = [sb(f"kT{i}", [128, TH], BF16) for i in range(4)]
        vT = [sb(f"vT{i}", [128, TH], BF16) for i in range(4)]
        g0 = sb("g0", [128, TH], F32)
        g1 = sb("g1", [128, TH], F32)
        rho16 = sb("rho16", [128, TH], BF16)
        e16 = sb("e16", [128, TH], BF16)
        ef16 = sb("ef16", [128, TH], BF16)
        sqt = [sb(f"sqt{i}", [128, 512], F32) for i in range(2)]
        nrm = sb("nrm", [1, 512], F32)
        wsb = [sb(f"wsb{i}", [128, 1024], BF16) for i in range(2)]
        wos = [sb(f"wos{i}", [128, 1024], BF16) for i in range(4)]
        strRO = [sb(f"strRO{i}", [128, 512], BF16) for i in range(2)]
        strNU = [sb(f"strNU{i}", [128, 512], BF16) for i in range(2)]
        strE1 = [sb(f"strE1{i}", [128, 512], BF16) for i in range(2)]
        strEF = [sb(f"strEF{i}", [128, 512], BF16) for i in range(2)]
        vtokP = [sb(f"vtokP{i}", [128, 512], BF16) for i in range(2)]
        Wst = [sb(f"Wst{i}", [128, 512], BF16) for i in range(4)]
        NRst = [sb(f"NRst{i}", [128, 512], BF16) for i in range(4)]
        PPst = [sb(f"PPst{i}", [128, 64], BF16) for i in range(3)]
        pmT = sb("pmT", [64, 64], BF16)
        ydve = sb("ydve", [128, 512], BF16)
        ytmp = sb("ytmp", [128, 512], BF16)
        stmp = sb("stmp", [128, 512], BF16)
        St = [sb(f"St{i}", [128, 512], BF16) for i in range(4)]
        Db = sb("Db", [128, 512], BF16)
        cG = [sb(f"cG{i}", [128, 448], BF16) for i in range(2)]
        cSN = [sb(f"cSN{i}", [128, 448], BF16) for i in range(2)]
        cSP = sb("cSP", [64, 288], BF16)
        cCM = [sb(f"cCM{i}", [128, 64], BF16) for i in range(3)]
        cTM = sb("cTM", [64, 64], BF16)
        cSC64 = sb("cSC64", [128, 2048], BF16)
        cSC8 = sb("cSC8", [128, 2048], BF16)
        cS63 = [sb(f"cS63{i}", [128, 128], BF16) for i in range(2)]
        gcol = {nm: sb("c_" + nm, [128, 4], F32)
                for nm in ["wgc", "bgc", "wlc", "blc"]}
        onecf = sb("onecf", [128, 1], F32)
        oner = sb("oner", [1, 128], F32)
        yTs = [sb(f"yTs{i}", [128, 512], BF16) for i in range(8)]
        fres = [sb(f"fres{i}", [128, 512], F32) for i in range(2)]

        S = Sems(nc, ctx, ["ld", "wd0", "wd1", "pe", "ssem", "vsem", "gsem",
                           "bnc", "lda", "ldb", "yst", "yld0", "yld1", "pst",
                           "scl"])
        block = ctx.enter_context(nc.Block())

        @block.gpsimd
        def _(g):
            g.memset(onecf[:], 1.0)
            g.memset(oner[:], 1.0)
            for i in range(4):
                g.memset(St[i][:], 0.0)
            g.memset(PPst[2][:], 0.0)
            S.inc(g.nop(), "gsem", 1)

        @block.sync
        def _(sy):
            for dst, src in [(cSP, cselph), (cTM, ctmh), (cSC64, csc64h),
                             (cSC8, csc8h)]:
                S.inc(sy.dma_start(out=dst[:], in_=src), "scl", 16)
            for i in range(2):
                S.inc(sy.dma_start(out=cG[i][:], in_=cgargh[i]), "scl", 16)
                S.inc(sy.dma_start(out=cSN[i][:], in_=cselnh[i]), "scl", 16)
                S.inc(sy.dma_start(out=cS63[i][:], in_=sel63h[i]), "scl", 16)
            for i in range(3):
                S.inc(sy.dma_start(out=cCM[i][:], in_=ccmh[i]), "scl", 16)
            for nm in gcol:
                S.inc(sy.dma_start(out=gcol[nm][:], in_=gc_h[nm]), "scl", 16)
            for jt in range(4):
                S.inc(sy.dma_start(out=wos[jt][:],
                                   in_=woT[128 * jt:128 * (jt + 1), :]), "scl", 16)
        scl_all = S.n["scl"]

        with ExitStack() as pctx:
            ps = lambda n, sh: pctx.enter_context(nc.psum_tensor(n, sh, F32)).ap()
            pX = [ps(f"pX{i}", [128, 512]) for i in range(4)]
            pY = ps("pY", [128, 512])
            pG = ps("pG", [128, 512])
            pDb = ps("pDb", [128, 512])
            pP = ps("pP", [128, 64])

            supd_end = {-1: 0}
            pe_chunk = {}
            pe_oproj = {}

            def emit_projT(w_hbm, dst):
                wd_0 = [S.n["wd0"], S.n["wd1"]]
                pe_0 = S.n["pe"]
                se_0 = S.n["ssem"]
                ld_in = S.n["ld"]

                @block.sync
                def _(sy):
                    S.wait(sy, "pe", pe_0)
                    for jt in range(4):
                        if jt >= 2:
                            S.wait(sy, "pe", pe_0 + 4 * (jt - 1))
                        wsem = "wd0" if jt % 2 == 0 else "wd1"
                        for kt in range(8):
                            S.inc(sy.dma_start(
                                out=wsb[jt % 2][:, 128 * kt:128 * (kt + 1)],
                                in_=w_hbm[128 * kt:128 * (kt + 1),
                                          128 * jt:128 * (jt + 1)]), wsem, 16)

                vs_0 = S.n["vsem"]

                @block.tensor
                def _(t):
                    S.wait(t, "ld", ld_in)
                    S.wait(t, "gsem", 1)
                    S.wait(t, "scl", scl_all)
                    S.wait(t, "ssem", se_0)
                    S.wait(t, "vsem", vs_0)
                    for j in range(16):
                        jt, ch = j // 4, j % 4
                        if ch == 0:
                            S.wait(t, "wd0" if jt % 2 == 0 else "wd1",
                                   wd_0[jt % 2] + 128 * (jt // 2 + 1))
                        if j >= 2:
                            S.wait(t, "ssem", se_0 + j - 1)
                        sl = slice(512 * ch, 512 * (ch + 1))
                        last = None
                        for kt in range(8):
                            last = t.matmul(pX[j % 2][:],
                                            wsb[jt % 2][:, 128 * kt:128 * (kt + 1)],
                                            xln[kt][:, sl], start=(kt == 0),
                                            stop=(kt == 7), skip_group_check=True)
                        S.inc(last, "pe", 1)

                @block.scalar
                def _(s):
                    for j in range(16):
                        jt, ch = j // 4, j % 4
                        S.wait(s, "pe", pe_0 + j + 1)
                        S.inc(s.activation(out=dst[jt][:, 512 * ch:512 * (ch + 1)],
                                           in_=pX[j % 2][:], func=AF.Copy),
                              "ssem", 1)

            for hp in range(2):
                t0 = TH * hp
                pe_pre = S.n["pe"]

                @block.gpsimd
                def _(g):
                    S.wait(g, "pe", pe_pre)
                    for i in range(8):
                        S.inc(g.dma_start(out=xln[i][:],
                                          in_=zTb[128 * i:128 * (i + 1),
                                                  t0:t0 + TH]), "ld", 16)

                emit_projT(wqT, qT)
                emit_projT(wkT, kT)
                emit_projT(wvT, vT)
                se_projv = S.n["ssem"]

                @block.gpsimd
                def _(g):
                    S.wait(g, "ssem", se_projv)
                    for jt in range(4):
                        S.inc(g.dma_start(out=vB[128 * jt:128 * (jt + 1),
                                                 t0:t0 + TH],
                                          in_=vT[jt][:]), "bnc", 16)

                for tc in range(4):
                    sl = slice(512 * tc, 512 * (tc + 1))
                    se_k0 = S.n["ssem"]
                    pe_k0 = S.n["pe"]
                    vs_k0 = S.n["vsem"]

                    @block.scalar
                    def _(s, sl=sl, se_k0=se_k0, pe_k0=pe_k0, vs_k0=vs_k0):
                        if se_k0 == se_projv:
                            S.wait(s, "ssem", se_projv)
                        S.wait(s, "vsem", vs_k0)
                        for jt in range(4):
                            if jt >= 2:
                                S.wait(s, "pe", pe_k0 + jt - 1)
                            S.inc(s.activation(out=sqt[jt % 2][:],
                                               in_=kT[jt][:, sl],
                                               func=AF.Square), "ssem", 1)

                    @block.tensor
                    def _(t, sl=sl, se_k0=se_k0, pe_k0=pe_k0, vs_k0=vs_k0):
                        S.wait(t, "vsem", vs_k0)
                        for jt in range(4):
                            S.wait(t, "ssem", se_k0 + jt + 1)
                            S.inc(t.matmul(pY[0:1, :], onecf[:], sqt[jt % 2][:],
                                           start=(jt == 0), stop=(jt == 3),
                                           skip_group_check=True), "pe", 1)

                    vs_cl = S.n["vsem"]

                    @block.vector
                    def _(v, pe_k0=pe_k0):
                        S.wait(v, "pe", pe_k0 + 4)
                        S.inc(v.tensor_scalar(nrm[:], pY[0:1, :], 1e-24, None,
                                              ALU.max), "vsem", 1)

                    ss_r = S.n["ssem"]

                    @block.scalar
                    def _(s, vs_cl=vs_cl):
                        S.wait(s, "vsem", vs_cl + 1)
                        s.activation(out=nrm[:], in_=nrm[:], func=AF.Ln)
                        S.inc(s.activation(out=nrm[:], in_=nrm[:], func=AF.Exp,
                                           scale=-0.5), "ssem", 1)

                    pe_bc = S.n["pe"]

                    @block.tensor
                    def _(t, ss_r=ss_r):
                        S.wait(t, "ssem", ss_r + 1)
                        S.inc(t.matmul(pG[:], oner[:], nrm[:], start=True,
                                       stop=True, skip_group_check=True), "pe", 1)

                    @block.vector
                    def _(v, sl=sl, pe_bc=pe_bc):
                        S.wait(v, "pe", pe_bc + 1)
                        for jt in range(4):
                            S.inc(v.tensor_tensor(kT[jt][:, sl], kT[jt][:, sl],
                                                  pG[:], ALU.mult), "vsem", 1)
                vs_knorm = S.n["vsem"]

                for jt in range(4):
                    vs_g0 = S.n["vsem"]
                    ss_g0 = S.n["ssem"]
                    bnc_pre = S.n["bnc"]

                    @block.scalar
                    def _(s, jt=jt, vs_g0=vs_g0):
                        S.wait(s, "vsem", vs_knorm)
                        S.wait(s, "vsem", vs_g0)
                        S.inc(s.activation(out=g0[:], in_=kT[jt][:],
                                           func=AF.Sigmoid,
                                           scale=gcol["wgc"][:, jt:jt + 1],
                                           bias=gcol["bgc"][:, jt:jt + 1]),
                              "ssem", 1)
                        S.inc(s.activation(out=g1[:], in_=kT[jt][:],
                                           func=AF.Sigmoid,
                                           scale=gcol["wlc"][:, jt:jt + 1],
                                           bias=gcol["blc"][:, jt:jt + 1]),
                              "ssem", 1)

                    @block.vector
                    def _(v, jt=jt, ss_g0=ss_g0, bnc_pre=bnc_pre):
                        S.wait(v, "ssem", ss_g0 + 2)
                        v.tensor_scalar(g1[:], g1[:], -1.0, 1.0, ALU.mult,
                                        ALU.add)
                        v.tensor_tensor(g0[:], g0[:], g1[:], ALU.mult)
                        v.tensor_scalar(g1[:], g0[:], -1.0, 1.0, ALU.mult,
                                        ALU.add)
                        S.wait(v, "bnc", bnc_pre)
                        S.inc(v.tensor_tensor(kT[jt][:], kT[jt][:], g1[:],
                                              ALU.mult), "vsem", 1)
                    ss_ln = S.n["ssem"]
                    vs_nu = S.n["vsem"]

                    @block.scalar
                    def _(s, vs_nu=vs_nu):
                        S.wait(s, "vsem", vs_nu)
                        S.inc(s.activation(out=g1[:], in_=g0[:], func=AF.Ln),
                              "ssem", 1)
                    vs_l0 = S.n["vsem"]

                    @block.vector
                    def _(v, ss_ln=ss_ln):
                        S.wait(v, "ssem", ss_ln + 1)
                        v.tensor_scalar(g0[:], g1[:], -11.0, None, ALU.max)
                        S.inc(v.tensor_scalar(rho16[:], g0[:], 0.0, None,
                                              ALU.add), "vsem", 1)
                        S.inc(v.tensor_tensor_scan(g1[:], cSC64[:], g0[:], 0.0,
                                                   ALU.mult, ALU.add), "vsem", 1)
                    ss_ef = S.n["ssem"]

                    @block.scalar
                    def _(s, vs_l0=vs_l0):
                        S.wait(s, "vsem", vs_l0 + 2)
                        S.inc(s.activation(out=ef16[:], in_=g1[:], func=AF.Exp),
                              "ssem", 1)

                    @block.vector
                    def _(v, ss_ef=ss_ef):
                        S.wait(v, "ssem", ss_ef + 1)
                        v.tensor_tensor(g0[:], g0[:], cSC8[:], ALU.mult)
                        S.inc(v.tensor_tensor_scan(g1[:], cSC8[:], g0[:], 0.0,
                                                   ALU.mult, ALU.add), "vsem", 1)
                    vs_e1 = S.n["vsem"]

                    @block.scalar
                    def _(s, vs_e1=vs_e1):
                        S.wait(s, "vsem", vs_e1)
                        S.inc(s.activation(out=e16[:], in_=g1[:], func=AF.Exp),
                              "ssem", 1)
                    ss_e1 = S.n["ssem"]

                    @block.gpsimd
                    def _(g, jt=jt, vs_nu=vs_nu, vs_l0=vs_l0, ss_ef=ss_ef,
                          ss_e1=ss_e1):
                        rsl = slice(128 * jt, 128 * (jt + 1))
                        S.wait(g, "vsem", vs_nu)
                        S.inc(g.dma_start(out=nuB[rsl, t0:t0 + TH],
                                          in_=kT[jt][:]), "bnc", 16)
                        S.wait(g, "vsem", vs_l0 + 1)
                        S.inc(g.dma_start(out=rhoB[rsl, t0:t0 + TH],
                                          in_=rho16[:]), "bnc", 16)
                        S.wait(g, "ssem", ss_ef + 1)
                        S.inc(g.dma_start(out=efB[rsl, t0:t0 + TH],
                                          in_=ef16[:]), "bnc", 16)
                        S.wait(g, "ssem", ss_e1)
                        S.inc(g.dma_start(out=e1B[rsl, t0:t0 + TH],
                                          in_=e16[:]), "bnc", 16)
                bnc_half = S.n["bnc"]
                se_proj_all = se_projv

                ld_par0 = [S.n["lda"], S.n["ldb"]]
                for pair in range(16):
                    rot = pair % 2
                    lsem = "lda" if rot == 0 else "ldb"
                    tg0 = t0 + 128 * pair

                    c_first = 32 * hp + 2 * pair

                    @block.sync
                    def _(sy, pair=pair, rot=rot, lsem=lsem, tg0=tg0,
                          c_first=c_first):
                        S.wait(sy, "bnc", bnc_half)
                        if pair >= 2:
                            S.wait(sy, "vsem", supd_end[c_first - 3])
                        for buf, src in [(strRO, rhoB), (strNU, nuB),
                                         (strE1, e1B), (strEF, efB),
                                         (vtokP, vB)]:
                            for jt in range(4):
                                S.inc(sy.dma_start(
                                    out=buf[rot][:, 128 * jt:128 * (jt + 1)],
                                    in_=src[128 * jt:128 * (jt + 1),
                                            tg0:tg0 + 128],
                                    transpose=True), lsem, 16)

                    for par in range(2):
                        c_idx = 32 * hp + 2 * pair + par
                        col0 = 128 * pair + 64 * par
                        csl = slice(col0, col0 + 64)
                        psl = slice(64 * par, 64 * par + 64)
                        ld_t = ld_par0[rot] + 320 * (pair // 2 + 1)
                        ss_c0 = S.n["ssem"]
                        vs_c0 = S.n["vsem"]
                        pe_c0 = S.n["pe"]

                        @block.tensor
                        def _(t, rot=rot, lsem=lsem, c_idx=c_idx, csl=csl,
                              psl=psl, ld_t=ld_t, ss_c0=ss_c0, vs_c0=vs_c0,
                              par=par):
                            S.wait(t, lsem, ld_t)
                            S.wait(t, "ssem", se_proj_all)
                            S.wait(t, "vsem", supd_end[c_idx - 1])
                            S.wait(t, "ssem", ss_c0)
                            for kt in range(4):
                                last = t.matmul(pP[0:64, 0:64],
                                                vT[kt][:, csl], qT[kt][:, csl],
                                                start=(kt == 0), stop=(kt == 3),
                                                skip_group_check=True)
                            S.inc(last, "pe", 1)
                            for m in range(3):
                                t.matmul(pX[m][:], cG[par][:, 128 * m:128 * (m + 1)],
                                         strRO[rot][:], start=True, stop=True,
                                         skip_group_check=True)
                            S.inc(t.matmul(pX[3][psl, :], cG[par][:, 384:448],
                                           strRO[rot][:], start=True, stop=True,
                                           skip_group_check=True), "pe", 1)
                            S.wait(t, "ssem", ss_c0 + 4)
                            for m in range(3):
                                t.matmul(pX[m][:], cSN[par][:, 128 * m:128 * (m + 1)],
                                         strNU[rot][:], start=True, stop=True,
                                         skip_group_check=True)
                            S.inc(t.matmul(pX[3][psl, :], cSN[par][:, 384:448],
                                           strNU[rot][:], start=True, stop=True,
                                           skip_group_check=True), "pe", 1)
                            S.wait(t, "vsem", vs_c0 + 5)
                            for m, sz in [(0, 128), (1, 128), (2, 32)]:
                                S.inc(t.matmul(pX[m][0:sz, 0:64],
                                               cSP[:, 128 * m:128 * m + sz],
                                               pmT[:], start=True, stop=True,
                                               skip_group_check=True), "pe", 1)
                            S.wait(t, "vsem", vs_c0 + 8)
                            for vt in range(4):
                                last = t.matmul(pG[psl, :], qT[vt][:, csl],
                                                St[vt][:], start=(vt == 0),
                                                stop=(vt == 3),
                                                skip_group_check=True)
                            S.inc(last, "pe", 1)
                            for m in range(3):
                                last = t.matmul(pY[psl, :], PPst[m][:],
                                                NRst[m][:], start=(m == 0),
                                                stop=(m == 2),
                                                skip_group_check=True)
                            S.inc(last, "pe", 1)
                            S.inc(t.matmul(pDb[:], cS63[par][:], strEF[rot][:],
                                           start=True, stop=True,
                                           skip_group_check=True), "pe", 1)
                            for vt in range(4):
                                S.inc(t.matmul(pX[vt][:],
                                               vtokP[rot][psl, 128 * vt:128 * (vt + 1)],
                                               NRst[3][psl, :], start=True,
                                               stop=True, skip_group_check=True),
                                      "pe", 1)

                        @block.scalar
                        def _(s, pe_c0=pe_c0, psl=psl, par=par, vs_c0=vs_c0):
                            S.wait(s, "pe", pe_c0 + 2)
                            S.wait(s, "vsem", vs_c0 - 8)
                            for m in range(3):
                                S.inc(s.activation(out=Wst[m][:], in_=pX[m][:],
                                                   func=AF.Exp), "ssem", 1)
                            S.inc(s.activation(out=Wst[3][psl, :],
                                               in_=pX[3][psl, :], func=AF.Exp),
                                  "ssem", 1)
                            S.wait(s, "pe", pe_c0 + 9)
                            S.wait(s, "vsem", vs_c0)
                            S.inc(s.activation(out=Db[:], in_=pDb[:],
                                               func=AF.Copy), "ssem", 1)

                        @block.vector
                        def _(v, pe_c0=pe_c0, ss_c0=ss_c0, vs_c0=vs_c0,
                              psl=psl, rot=rot, c_idx=c_idx):
                            S.wait(v, "pe", pe_c0 + 1)
                            S.inc(v.tensor_tensor(pmT[:], pP[0:64, 0:64],
                                                  cTM[:], ALU.mult), "vsem", 1)
                            S.wait(v, "pe", pe_c0 + 3)
                            S.wait(v, "ssem", ss_c0 + 4)
                            for m in range(3):
                                S.inc(v.tensor_tensor(NRst[m][:], pX[m][:],
                                                      Wst[m][:], ALU.mult),
                                      "vsem", 1)
                            S.inc(v.tensor_tensor(NRst[3][psl, :], pX[3][psl, :],
                                                  Wst[3][psl, :], ALU.mult),
                                  "vsem", 1)
                            S.wait(v, "pe", pe_c0 + 6)
                            for m, sz in [(0, 128), (1, 128), (2, 32)]:
                                S.inc(v.tensor_tensor(PPst[m][0:sz, :],
                                                      pX[m][0:sz, 0:64],
                                                      cCM[m][0:sz, :], ALU.mult),
                                      "vsem", 1)
                            S.wait(v, "pe", pe_c0 + 8)
                            if c_idx >= 2:
                                S.wait(v, "yst", 16 * (c_idx - 1))
                            v.tensor_tensor(ydve[psl, :], strEF[rot][psl, :],
                                            pG[psl, :], ALU.mult)
                            v.tensor_tensor(ytmp[psl, :], strE1[rot][psl, :],
                                            pY[psl, :], ALU.mult)
                            S.inc(v.tensor_tensor(ydve[psl, :], ydve[psl, :],
                                                  ytmp[psl, :], ALU.add),
                                  "vsem", 1)
                            S.wait(v, "pe", pe_c0 + 13)
                            S.wait(v, "ssem", ss_c0 + 5)
                            for vt in range(4):
                                v.tensor_tensor(stmp[:], St[vt][:], Db[:],
                                                ALU.mult)
                                S.inc(v.tensor_tensor(St[vt][:], stmp[:],
                                                      pX[vt][:], ALU.add),
                                      "vsem", 1)
                        supd_end[c_idx] = S.n["vsem"]
                        pe_chunk[c_idx] = S.n["pe"]
                        vs_y = supd_end[c_idx] - 4

                        @block.gpsimd
                        def _(g, c_idx=c_idx, psl=psl, vs_y=vs_y):
                            S.wait(g, "vsem", vs_y)
                            S.inc(g.dma_start(out=yB[64 * c_idx:64 * (c_idx + 1), :],
                                              in_=ydve[psl, :]), "yst", 16)

            yst_all = S.n["yst"]
            yld0 = [S.n["yld0"], S.n["yld1"]]
            st_idx = 0
            for tch in range(8):
                rot = tch % 2
                lsem = "yld0" if rot == 0 else "yld1"

                @block.sync
                def _(sy, tch=tch, rot=rot, lsem=lsem):
                    S.wait(sy, "yst", yst_all)
                    if tch >= 2:
                        S.wait(sy, "pe", pe_oproj[tch - 2])
                    for jt in range(4):
                        for ts_ in range(4):
                            S.inc(sy.dma_start(
                                out=yTs[4 * rot + jt][:, 128 * ts_:128 * (ts_ + 1)],
                                in_=yB[512 * tch + 128 * ts_:512 * tch + 128 * (ts_ + 1),
                                       128 * jt:128 * (jt + 1)],
                                transpose=True), lsem, 16)

                ld_t = yld0[rot] + 256 * (tch // 2 + 1)
                for ot in range(8):
                    ss_o0 = S.n["ssem"]
                    pe_o0 = S.n["pe"]

                    @block.tensor
                    def _(t, tch=tch, rot=rot, lsem=lsem, ot=ot, ld_t=ld_t,
                          ss_o0=ss_o0):
                        S.wait(t, lsem, ld_t)
                        if tch == 0 and ot == 0:
                            S.wait(t, "vsem", supd_end[63])
                        if ot >= 2 or tch > 0:
                            S.wait(t, "ssem", ss_o0 - 1)
                        for jt in range(4):
                            last = t.matmul(pX[ot % 2][:],
                                            wos[jt][:, 128 * ot:128 * (ot + 1)],
                                            yTs[4 * rot + jt][:],
                                            start=(jt == 0), stop=(jt == 3),
                                            skip_group_check=True)
                        S.inc(last, "pe", 1)

                    @block.scalar
                    def _(s, ot=ot, pe_o0=pe_o0, st_idx=st_idx):
                        S.wait(s, "pe", pe_o0 + 1)
                        if st_idx >= 2:
                            S.wait(s, "pst", 16 * (st_idx - 1))
                        S.inc(s.activation(out=fres[ot % 2][:],
                                           in_=pX[ot % 2][:], func=AF.Copy),
                              "ssem", 1)
                    ss_ev = S.n["ssem"]

                    @block.gpsimd
                    def _(g, tch=tch, ot=ot, ss_ev=ss_ev):
                        S.wait(g, "ssem", ss_ev)
                        S.inc(g.dma_start(
                            out=poT[128 * ot:128 * (ot + 1),
                                    512 * tch:512 * (tch + 1)],
                            in_=fres[ot % 2][:]), "pst", 16)
                    st_idx += 1
                pe_oproj[tch] = S.n["pe"]

            pst_all = S.n["pst"]

            @block.vector
            def _(v):
                S.wait(v, "pst", pst_all)
                v.nop()
    return nc

def _emit_token(nc, ctx, block, src_f32, tensor_tok=True):
    tgs = ctx.enter_context(nc.semaphore("tok_g"))
    tds = ctx.enter_context(nc.semaphore("tok_d"))
    tps = ctx.enter_context(nc.semaphore("tok_p"))
    tb = ctx.enter_context(nc.sbuf_tensor("tok_b", [1, 8], BF16)).ap()
    tf = ctx.enter_context(nc.sbuf_tensor("tok_f", [1, 8], F32)).ap()
    tf2 = ctx.enter_context(nc.sbuf_tensor("tok_f2", [1, 8], F32)).ap()

    @block.gpsimd
    def _(g):
        g.memset(tb[:], 1.0)
        g.nop().then_inc(tgs, 1)

    @block.sync
    def _(sy):
        sy.wait_ge(tgs, 1)
        sy.dma_start(out=tf[:], in_=src_f32[0:1, 0:8]).then_inc(tds, 16)
        sy.wait_ge(tds, 16)
    if tensor_tok:
        ps = ctx.enter_context(nc.psum_tensor("tok_ps", [8, 8], F32)).ap()

        @block.tensor
        def _(t):
            t.wait_ge(tgs, 1)
            t.matmul(ps[:], tb[:], tb[:], start=True, stop=True,
                     skip_group_check=True).then_inc(tps, 1)

        @block.scalar
        def _(s):
            s.wait_ge(tps, 1)
            s.activation(out=tf2[:], in_=ps[0:1, :], func=AF.Copy)


def build_combine3():
    nc = bass.Bass("TRN2", target_bir_lowering=False, detect_race_conditions=False)
    xT = nc.dram_tensor("xT", [D, TLOC], F32, kind="ExternalInput").ap()
    pa = nc.dram_tensor("pa", [D, TLOC], F32, kind="ExternalInput").ap()
    pb = nc.dram_tensor("pb", [D, TLOC], F32, kind="ExternalInput").ap()
    yT = nc.dram_tensor("yT", [D, TLOC], F32, kind="ExternalOutput").ap()
    with ExitStack() as ctx:
        sb = lambda n, sh, dt: ctx.enter_context(nc.sbuf_tensor(n, sh, dt)).ap()
        ba = [sb(f"ba{i}", [128, TLOC], F32) for i in range(2)]
        bb = [sb(f"bb{i}", [128, TLOC], F32) for i in range(2)]
        bc = [sb(f"bc{i}", [128, TLOC], F32) for i in range(2)]
        S = Sems(nc, ctx, ["ld", "vsem", "st"])
        block = ctx.enter_context(nc.Block())
        _emit_token(nc, ctx, block, xT)
        for i in range(8):
            r = i % 2
            sl = slice(128 * i, 128 * (i + 1))
            ld0 = S.n["ld"]

            @block.gpsimd
            def _(g, i=i, r=r, sl=sl):
                if i >= 2:
                    S.wait(g, "st", 16 * (i - 1))
                S.inc(g.dma_start(out=ba[r][:], in_=xT[sl, :]), "ld", 16)
                S.inc(g.dma_start(out=bb[r][:], in_=pa[sl, :]), "ld", 16)
                S.inc(g.dma_start(out=bc[r][:], in_=pb[sl, :]), "ld", 16)

            @block.vector
            def _(v, r=r, ld0=ld0):
                S.wait(v, "ld", ld0 + 48)
                v.tensor_tensor(ba[r][:], ba[r][:], bb[r][:], ALU.add)
                S.inc(v.tensor_tensor(ba[r][:], ba[r][:], bc[r][:], ALU.add),
                      "vsem", 1)
            vs = S.n["vsem"]

            @block.gpsimd
            def _(g, r=r, sl=sl, vs=vs):
                S.wait(g, "vsem", vs)
                S.inc(g.dma_start(out=yT[sl, :], in_=ba[r][:]), "st", 16)

        @block.vector
        def _(v):
            S.wait(v, "st", S.n["st"])
            v.nop()
    return nc


def build_prep(width):
    assert width % 10 == 0
    W = width // 10
    nc = bass.Bass("TRN2", target_bir_lowering=False, detect_race_conditions=False)
    fin = nc.dram_tensor("fin", [128, width], F32, kind="ExternalInput").ap()
    fout = nc.dram_tensor("fout", [128, width], BF16, kind="ExternalOutput").ap()
    with ExitStack() as ctx:
        sb = lambda n, sh, dt: ctx.enter_context(nc.sbuf_tensor(n, sh, dt)).ap()
        buf = [sb(f"buf{i}", [128, W], BF16) for i in range(2)]
        tb = sb("tb", [1, 8], BF16)
        ts_ = sb("ts_", [1, 8], F32)
        tf = sb("tf", [1, 8], F32)
        ps = ctx.enter_context(nc.psum_tensor("ps", [8, 8], F32)).ap()
        S = Sems(nc, ctx, ["ld", "st", "gsem", "pe", "ssem", "scl"])
        block = ctx.enter_context(nc.Block())

        @block.gpsimd
        def _(g):
            g.memset(tb[:], 1.0)
            S.inc(g.nop(), "gsem", 1)

        @block.sync
        def _(sy):
            S.wait(sy, "gsem", 1)
            S.inc(sy.dma_start(out=tf[:], in_=fin[0:1, 0:8]), "scl", 16)

        @block.tensor
        def _(t):
            S.wait(t, "gsem", 1)
            S.inc(t.matmul(ps[0:8, :], tb[:], tb[:], start=True, stop=True,
                           skip_group_check=True), "pe", 1)

        @block.scalar
        def _(s):
            S.wait(s, "pe", 1)
            S.inc(s.activation(out=ts_[:], in_=ps[0:1, :], func=AF.Copy),
                  "ssem", 1)

        for k in range(10):
            sl = slice(W * k, W * (k + 1))

            @block.gpsimd
            def _(g, k=k, sl=sl):
                if k == 0:
                    S.wait(g, "scl", 16)
                    S.wait(g, "ssem", 1)
                if k >= 2:
                    S.wait(g, "st", 16 * (k - 1))
                S.inc(g.dma_start(out=buf[k % 2][:], in_=fin[:, sl]), "ld", 16)
                S.wait(g, "ld", 16 * (k + 1))
                S.inc(g.dma_start(out=fout[:, sl], in_=buf[k % 2][:]), "st", 16)

        @block.vector
        def _(v):
            S.wait(v, "st", 160)
            v.nop()
    return nc


def build_ln_tok():
    nc = bass.Bass("TRN2", target_bir_lowering=False, detect_race_conditions=False)
    xT = nc.dram_tensor("xT", [D, TLOC], F32, kind="ExternalInput").ap()
    idn = nc.dram_tensor("idn", [128, 128], BF16, kind="ExternalInput").ap()
    zT = nc.dram_tensor("zT", [D, TLOC], F32, kind="ExternalOutput").ap()
    zTOK = nc.dram_tensor("zTOK", [TLOC, 2 * D], BF16, kind="ExternalOutput").ap()
    NT = 1024
    with ExitStack() as ctx:
        sb = lambda n, sh, dt: ctx.enter_context(nc.sbuf_tensor(n, sh, dt))
        xln = [sb(f"xln{i}", [128, NT], BF16) for i in range(8)]
        zf = [sb(f"zf{i}", [128, NT], F32) for i in range(8)]
        zfb = [sb(f"zfb{i}", [128, TLOC], BF16) for i in range(8)]
        sq = [sb(f"sq{i}", [128, NT], BF16) for i in range(2)]
        xf = [sb(f"xf{i}", [128, NT], F32) for i in range(2)]
        rb = sb("rb", [128, NT], F32)
        ab = sb("ab", [128, NT], F32)
        stat = sb("stat", [1, 3 * NT], F32)
        onec = sb("onec", [128, 1], BF16)
        oner = sb("oner", [1, 128], F32)
        sidn = sb("sidn", [128, 128], BF16)
        ztk = [sb(f"ztk{i}", [128, 2048], BF16) for i in range(2)]
        S = Sems(nc, ctx, ["ld", "lda", "ldb", "pe", "ssem", "vsem", "gsem",
                           "sta", "stb", "stc", "vfree", "wd0", "wd1", "scl"])
        block = ctx.enter_context(nc.Block())

        @block.gpsimd
        def _(g):
            g.memset(onec[:], 1.0)
            g.memset(oner[:], 1.0)
            g.memset(ztk[0][:], 0.0)
            g.memset(ztk[1][:], 0.0)
            S.inc(g.nop(), "gsem", 1)

        @block.sync
        def _(sy):
            S.inc(sy.dma_start(out=sidn[:], in_=idn), "scl", 16)
        scl_idn = S.n["scl"]

        with ExitStack() as pctx:
            pg = [pctx.enter_context(nc.psum_tensor(f"pg{i}", [128, NT], F32))
                  for i in range(4)]
            prev = {}
            for half in range(2):
                t0 = half * NT
                bufs = dict(xln=[t[:] for t in xln], sq=sq, xf=xf, stat=stat,
                            rb=rb, ab=ab, onec=onec, oner=oner, pg=pg)
                v_norm = emit_ln(nc, block, S, bufs, xT[:, t0:t0 + NT], NT,
                                 prev_gates=prev,
                                 fp32_out=[zf[i][:] for i in range(8)])
                v_base = v_norm - 8

                @block.gpsimd
                def _(g, t0=t0, v_base=v_base):
                    for i in range(8):
                        S.wait(g, "vsem", v_base + i + 1)
                        S.inc(g.dma_start(out=zT[128 * i:128 * (i + 1),
                                                 t0:t0 + NT],
                                          in_=zf[i][:]),
                              "sta" if i % 2 == 0 else "stb", 16)

                @block.scalar
                def _(s, t0=t0, v_base=v_base):
                    for i in range(8):
                        S.wait(s, "vsem", v_base + i + 1)
                        S.inc(s.activation(out=zfb[i][:, t0:t0 + NT],
                                           in_=zf[i][:], func=AF.Copy),
                              "ssem", 1)
                prev = {"pe": S.n["pe"], "ssem": S.n["ssem"],
                        "vsem": S.n["vsem"], "sta": S.n["sta"],
                        "stb": S.n["stb"]}
        ss_all = S.n["ssem"]

        with ExitStack() as pctx:
            ptk = [pctx.enter_context(nc.psum_tensor(f"ptk{i}", [128, 128],
                                                     BF16)) for i in range(2)]
            for jg in range(16):
                pe_t0 = S.n["pe"]
                vs_t0 = S.n["vsem"]

                @block.tensor
                def _(t, jg=jg, vs_t0=vs_t0):
                    S.wait(t, "scl", scl_idn)
                    S.wait(t, "ssem", ss_all)
                    for i in range(8):
                        if jg + i > 0:
                            S.wait(t, "vsem", vs_t0 + i - 1)
                        S.inc(t.transpose(ptk[i % 2][:],
                                          zfb[i][:, 128 * jg:128 * (jg + 1)],
                                          sidn[:]), "pe", 1)

                @block.vector
                def _(v, jg=jg, pe_t0=pe_t0):
                    if jg == 0:
                        S.wait(v, "gsem", 1)
                    if jg >= 2:
                        S.wait(v, "stc", 16 * (jg - 1))
                    for i in range(8):
                        S.wait(v, "pe", pe_t0 + i + 1)
                        v.tensor_copy(ztk[jg % 2][:, 256 * i:256 * i + 64],
                                      ptk[i % 2][0:128, 0:64])
                        S.inc(v.tensor_copy(
                            ztk[jg % 2][:, 256 * i + 128:256 * i + 192],
                            ptk[i % 2][0:128, 64:128]), "vsem", 1)
                vs_t1 = S.n["vsem"]

                @block.gpsimd
                def _(g, jg=jg, vs_t1=vs_t1):
                    S.wait(g, "vsem", vs_t1)
                    S.inc(g.dma_start(out=zTOK[128 * jg:128 * (jg + 1), :],
                                      in_=ztk[jg % 2][:]), "stc", 16)

        @block.vector
        def _(v):
            S.wait(v, "sta", S.n["sta"])
            S.wait(v, "stb", S.n["stb"])
            S.wait(v, "stc", S.n["stc"])
            v.nop()
    return nc


def build_l1attna():
    nc = bass.Bass("TRN2", target_bir_lowering=False, detect_race_conditions=False)
    zTK = nc.dram_tensor("zTK", [16 * TLOC, 128], BF16, kind="ExternalInput").ap()
    sigr = nc.dram_tensor("sigr", [1, 1], F32, kind="ExternalInput").ap()
    idn = nc.dram_tensor("idn", [128, 128], BF16, kind="ExternalInput").ap()
    bdmaskh = nc.dram_tensor("bdmaskh", [128, 128], BF16, kind="ExternalInput").ap()
    negh_h = nc.dram_tensor("negh_h", [1, 128], BF16, kind="ExternalInput").ap()
    oneb64h = nc.dram_tensor("oneb64h", [1, 128], BF16, kind="ExternalInput").ap()
    onec64h = nc.dram_tensor("onec64h", [128, 1], BF16, kind="ExternalInput").ap()
    oneb1h = nc.dram_tensor("oneb1h", [1, 1], BF16, kind="ExternalInput").ap()
    yA = nc.dram_tensor("yA", [16 * TLOC, 64], BF16, kind="ExternalOutput").ap()

    with ExitStack() as ctx:
        sb = lambda n, sh, dt: ctx.enter_context(nc.sbuf_tensor(n, sh, dt)).ap()
        Adm = [sb(f"Adm{i}", [128, 512], BF16) for i in range(2)]
        sqA = sb("sqA", [128, 512], BF16)
        nrowb = sb("nrowb", [1, 512], BF16)
        efull = sb("efull", [128, 128], BF16)
        Eblk = sb("Eblk", [128, 128], BF16)
        uT8 = sb("uT8", [128, 128], BF16)
        mix8 = [sb(f"mix8{i}", [128, 128], BF16) for i in range(2)]
        sgf = sb("sgf", [1, 1], F32)
        sigb = sb("sigb", [1, 1], BF16)
        sig2c = sb("sig2c", [128, 1], F32)
        sidn = sb("sidn", [128, 128], BF16)
        bdm = sb("bdm", [128, 128], BF16)
        negh = sb("negh", [1, 128], BF16)
        onec64 = sb("onec64", [128, 1], BF16)
        S = Sems(nc, ctx, ["lda", "ldb", "pe", "ssem", "vsem", "yst", "scl"])
        block = ctx.enter_context(nc.Block())

        @block.sync
        def _(sy):
            for dst, src in [(sidn, idn), (bdm, bdmaskh), (negh, negh_h),
                             (onec64, onec64h), (sgf, sigr)]:
                S.inc(sy.dma_start(out=dst[:], in_=src), "scl", 16)
        scl_all = S.n["scl"]

        with ExitStack() as pctx:
            ps = lambda n, sh, dt: pctx.enter_context(
                nc.psum_tensor(n, sh, dt)).ap()
            pNr = ps("pNr", [1, 512], F32)
            pG8 = [ps(f"pG8_{i}", [128, 128], F32) for i in range(2)]
            pUT = [ps(f"pUT{i}", [128, 128], BF16) for i in range(2)]
            pW8 = [ps(f"pW8_{i}", [128, 128], F32) for i in range(2)]
            pDN = ps("pDN", [128, 1], F32)

            @block.scalar
            def _(s):
                S.wait(s, "scl", scl_all)
                S.inc(s.activation(out=sigb[:], in_=sgf[:], func=AF.Copy),
                      "ssem", 1)

            @block.tensor
            def _(t):
                S.wait(t, "ssem", 1)
                S.inc(t.matmul(pDN[:], negh[:], sigb[:], start=True, stop=True,
                               skip_group_check=True), "pe", 1)

            @block.vector
            def _(v):
                S.wait(v, "pe", 1)
                S.inc(v.tensor_scalar(sig2c[:], pDN[:], -4.0, None, ALU.mult),
                      "vsem", 1)
            pe_g = {}
            vs_sig = S.n["vsem"]

            for grp in range(64):
                rot = grp % 2
                lsem = "lda" if rot == 0 else "ldb"
                r0 = 16 * 32 * grp
                ld_t = 64 * (grp // 2 + 1)

                @block.sync
                def _(sy, grp=grp, rot=rot, lsem=lsem, r0=r0):
                    if grp >= 2:
                        S.wait(sy, "pe", pe_g[grp - 2])
                    for g in range(4):
                        S.inc(sy.dma_start(
                            out=Adm[rot][:, 128 * g:128 * (g + 1)],
                            in_=zTK[r0 + 128 * g:r0 + 128 * (g + 1), :],
                            transpose=True), lsem, 16)

                vs_0 = S.n["vsem"]
                pe_0 = S.n["pe"]
                ss_0 = S.n["ssem"]

                @block.vector
                def _(v, rot=rot, lsem=lsem, ld_t=ld_t, grp=grp, ss_0=ss_0):
                    S.wait(v, lsem, ld_t)
                    S.wait(v, "vsem", vs_sig)
                    if grp >= 1:
                        S.wait(v, "ssem", ss_0)
                    S.inc(v.tensor_tensor(sqA[:], Adm[rot][:], Adm[rot][:],
                                          ALU.mult), "vsem", 1)

                @block.tensor
                def _(t, vs_0=vs_0, grp=grp, ss_0=ss_0):
                    S.wait(t, "vsem", vs_0 + 1)
                    if grp >= 1:
                        S.wait(t, "ssem", ss_0)
                    S.inc(t.matmul(pNr[:], onec64[:], sqA[:], start=True,
                                   stop=True, skip_group_check=True), "pe", 1)

                @block.scalar
                def _(s, pe_0=pe_0):
                    S.wait(s, "pe", pe_0 + 1)
                    S.inc(s.activation(out=nrowb[:], in_=pNr[:], func=AF.Copy),
                          "ssem", 1)
                ss_nrow = S.n["ssem"]

                for g in range(4):
                    gsl = slice(128 * g, 128 * (g + 1))
                    pr = g % 2
                    pe_1 = S.n["pe"]
                    ss_1 = S.n["ssem"]
                    vs_1 = S.n["vsem"]

                    @block.tensor
                    def _(t, rot=rot, gsl=gsl, ss_1=ss_1, vs_1=vs_1, g=g,
                          pr=pr, grp=grp, ss_nrow=ss_nrow):
                        gi = 4 * grp + g
                        S.wait(t, "ssem", ss_nrow)
                        if gi >= 2:
                            S.wait(t, "ssem", ss_1 - 1)
                            S.wait(t, "vsem", vs_1 - 3)
                        t.matmul(pG8[pr][:], Adm[rot][:, gsl], Adm[rot][:, gsl],
                                 start=True, stop=False, skip_group_check=True)
                        t.matmul(pG8[pr][:], negh[:], nrowb[0:1, gsl],
                                 start=False, stop=False, skip_group_check=True)
                        S.inc(t.matmul(pG8[pr][:], nrowb[0:1, gsl], negh[:],
                                       start=False, stop=True,
                                       skip_group_check=True), "pe", 1)
                        S.inc(t.transpose(pUT[pr][:], Adm[rot][:, gsl],
                                          sidn[:]), "pe", 1)

                    @block.scalar
                    def _(s, pe_1=pe_1, vs_1=vs_1, pr=pr):
                        S.wait(s, "pe", pe_1 + 1)
                        S.wait(s, "vsem", vs_1 - 1)
                        S.inc(s.activation(out=efull[:], in_=pG8[pr][:],
                                           func=AF.Exp, scale=sig2c[:, 0:1]),
                              "ssem", 1)

                    @block.vector
                    def _(v, pe_1=pe_1, ss_1=ss_1, pr=pr, grp=grp, g=g):
                        gi = 4 * grp + g
                        if gi >= 1:
                            S.wait(v, "pe", pe_1)
                        S.wait(v, "ssem", ss_1 + 1)
                        S.inc(v.tensor_tensor(Eblk[:], efull[:], bdm[:],
                                              ALU.mult), "vsem", 1)
                        S.wait(v, "pe", pe_1 + 2)
                        S.inc(v.tensor_copy(uT8[:], pUT[pr][:]), "vsem", 1)

                    pe_2 = S.n["pe"]

                    @block.tensor
                    def _(t, vs_1=vs_1, pr=pr):
                        S.wait(t, "vsem", vs_1 + 2)
                        S.inc(t.matmul(pW8[pr][:], Eblk[:], uT8[:], start=True,
                                       stop=True, skip_group_check=True),
                              "pe", 1)
                    mi = (4 * grp + g) % 2
                    yst_pre = 16 * (4 * grp + g - 1)

                    @block.vector
                    def _(v, pe_2=pe_2, mi=mi, yst_pre=yst_pre, pr=pr):
                        S.wait(v, "pe", pe_2 + 1)
                        if yst_pre > 16:
                            S.wait(v, "yst", yst_pre - 16)
                        S.inc(v.tensor_copy(mix8[mi][:], pW8[pr][:]), "vsem", 1)
                    vs_3 = S.n["vsem"]

                    @block.gpsimd
                    def _(g_, r0=r0, g=g, mi=mi, vs_3=vs_3):
                        S.wait(g_, "vsem", vs_3)
                        S.inc(g_.dma_start(
                            out=yA[r0 + 128 * g:r0 + 128 * (g + 1), :],
                            in_=mix8[mi][:, 0:64]), "yst", 16)
                pe_g[grp] = S.n["pe"]

            @block.vector
            def _(v):
                S.wait(v, "yst", S.n["yst"])
                v.nop()
    return nc


def build_combA():
    nc = bass.Bass("TRN2", target_bir_lowering=False, detect_race_conditions=False)
    xT = nc.dram_tensor("xT", [D, TLOC], F32, kind="ExternalInput").ap()
    zT = nc.dram_tensor("zT", [D, TLOC], F32, kind="ExternalInput").ap()
    yAc = nc.dram_tensor("yAc", [TLOC, D], BF16, kind="ExternalInput").ap()
    yTo = nc.dram_tensor("yTo", [D, TLOC], F32, kind="ExternalOutput").ap()
    with ExitStack() as ctx:
        sb = lambda n, sh, dt: ctx.enter_context(nc.sbuf_tensor(n, sh, dt)).ap()
        ba = [sb(f"ba{i}", [128, TLOC], F32) for i in range(2)]
        bb = [sb(f"bb{i}", [128, TLOC], F32) for i in range(2)]
        bm = [sb(f"bm{i}", [128, TLOC], BF16) for i in range(2)]
        S = Sems(nc, ctx, ["ld", "lda", "ldb", "vsem", "st"])
        block = ctx.enter_context(nc.Block())
        _emit_token(nc, ctx, block, xT)
        for i in range(8):
            r = i % 2
            sl = slice(128 * i, 128 * (i + 1))
            ld0 = S.n["ld"]
            lsem = "lda" if r == 0 else "ldb"
            lt0 = S.n[lsem]

            @block.gpsimd
            def _(g, i=i, r=r, sl=sl):
                if i >= 2:
                    S.wait(g, "st", 16 * (i - 1))
                S.inc(g.dma_start(out=ba[r][:], in_=xT[sl, :]), "ld", 16)
                S.inc(g.dma_start(out=bb[r][:], in_=zT[sl, :]), "ld", 16)

            @block.sync
            def _(sy, i=i, r=r, sl=sl, lsem=lsem):
                if i >= 2:
                    S.wait(sy, "st", 16 * (i - 1))
                for j in range(16):
                    S.inc(sy.dma_start(out=bm[r][:, 128 * j:128 * (j + 1)],
                                       in_=yAc[128 * j:128 * (j + 1), sl],
                                       transpose=True), lsem, 16)

            @block.vector
            def _(v, r=r, ld0=ld0, lsem=lsem, lt0=lt0):
                S.wait(v, "ld", ld0 + 32)
                S.wait(v, lsem, lt0 + 256)
                v.tensor_tensor(ba[r][:], ba[r][:], bb[r][:], ALU.add)
                S.inc(v.tensor_tensor(ba[r][:], ba[r][:], bm[r][:], ALU.add),
                      "vsem", 1)
            vs = S.n["vsem"]

            @block.gpsimd
            def _(g, r=r, sl=sl, vs=vs):
                S.wait(g, "vsem", vs)
                S.inc(g.dma_start(out=yTo[sl, :], in_=ba[r][:]), "st", 16)

        @block.vector
        def _(v):
            S.wait(v, "st", S.n["st"])
            v.nop()
    return nc

_PROGS = {}
_TRACE = None


def _tr(name, xc):
    if _TRACE is not None:
        _TRACE.append((name, np.concatenate([np.asarray(t).T for t in xc], 0)))


def _prog(key, fn):
    if key not in _PROGS:
        _PROGS[key] = fn()
    return _PROGS[key]


def _colmaj(s, n):
    return np.ascontiguousarray(np.asarray(s, np.float32).reshape(n, 128).T)


def _prestrip(w, n_ot, nkt):
    return np.ascontiguousarray(
        np.asarray(w, np.float32).reshape(n_ot, 128, nkt, 128)
        .transpose(0, 3, 2, 1).reshape(n_ot, 128, nkt * 128))


def _rotary_tables(pos, n_ctx=8192, hd=64, window=512):
    half = np.arange(0, hd, 2).astype(np.float32)
    inv = 1.0 / 10000.0 ** (half / hd)
    fr = pos[:, None] * inv[None, :]
    fr = np.concatenate([fr, fr], -1)
    base = (half + 0.4 * hd) / (1.4 * hd)
    power = (pos - n_ctx // 2) / (window // 2)
    sc = base[None, :] ** power[:, None]
    sc = np.concatenate([sc, sc], -1)
    return np.cos(fr), np.sin(fr), sc


def kernel(**inputs):
    import ml_dtypes
    bf = lambda a: np.ascontiguousarray(a).astype(ml_dtypes.bfloat16)
    f32 = lambda a: np.ascontiguousarray(np.asarray(a, np.float32))
    I = {k: np.asarray(v) for k, v in inputs.items()}

    x0 = f32(I["L0_attn_a_x"])
    idn = bf(np.eye(128, dtype=np.float32))

    def run8(nc, ims):
        return run_bass_kernel_spmd(nc, ims, core_ids=list(range(8)))

    Wq_b, Wk_b, Wv_b, Wo_b = (f32(I["L0_attn_b_Wq"]), f32(I["L0_attn_b_Wk"]),
                              f32(I["L0_attn_b_Wv"]), f32(I["L0_attn_b_Wo"]))
    bWq, bWkv = f32(I["L1_attn_b_Wq"]), f32(I["L1_attn_b_Wkv"])
    bWk, bWv, bWo = f32(I["L1_attn_b_Wk"]), f32(I["L1_attn_b_Wv"]), f32(I["L1_attn_b_Wo"])
    wkbp = np.zeros((128, 1024), np.float32)
    for h in range(16):
        c0 = 128 * (h // 2) + 64 * (h % 2)
        wkbp[:, c0:c0 + 32] = bWk[:, 32 * h:32 * (h + 1)]
    wob = np.zeros((512, 1024), np.float32)
    for h in range(16):
        wob[32 * h:32 * (h + 1)] = bWo[64 * h:64 * h + 32]
    HID = 2730
    w1p_l, w2p_l = [], []
    for l in range(2):
        w1 = f32(I["L1_ffn_w1"][l])
        w1p = np.zeros((1024, 5632), np.float32)
        w1p[:, 0:HID] = w1[:, 0:HID]
        w1p[:, 2816:2816 + HID] = w1[:, HID:]
        w1p_l.append(w1p)
        w2p_l.append(np.vstack([f32(I["L1_ffn_w2"][l]),
                                np.zeros((2816 - HID, 1024), np.float32)]))
    cast_list = [
        ("wqkv_a", f32(I["L0_attn_a_W_qkv"])),
        ("wout_a", f32(I["L0_attn_a_W_out"])),
        ("w1_w", _prestrip(I["L0_ffn_w1_w"], 32, 8)),
        ("w2_w", _prestrip(I["L0_ffn_w2_w"], 32, 8)),
        ("h1_w", _prestrip(I["L0_ffn_h1_w"], 32, 32)),
        ("h2_w", _prestrip(I["L0_ffn_h2_w"], 32, 32)),
        ("w3_w", _prestrip(I["L0_ffn_w3_w"], 8, 32)),
        ("ab_q0", np.ascontiguousarray(Wq_b[0:512, :].T)),
        ("ab_k0", np.ascontiguousarray(Wk_b[0:512, :].T)),
        ("ab_v0", np.ascontiguousarray(Wv_b[0:512, :].T)),
        ("ab_o0", np.ascontiguousarray(Wo_b[:, 0:512].T)),
        ("ab_q1", np.ascontiguousarray(Wq_b[512:, :].T)),
        ("ab_k1", np.ascontiguousarray(Wk_b[512:, :].T)),
        ("ab_v1", np.ascontiguousarray(Wv_b[512:, :].T)),
        ("ab_o1", np.ascontiguousarray(Wo_b[:, 512:].T)),
        ("bb_q", bWq), ("bb_kv", bWkv), ("bb_kbp", wkbp), ("bb_v", bWv),
        ("bb_ob", wob),
        ("at_qkv0", f32(I["L1_ffn_wqkv"][0])), ("at_wo0", f32(I["L1_ffn_wo"][0])),
        ("at_qkv1", f32(I["L1_ffn_wqkv"][1])), ("at_wo1", f32(I["L1_ffn_wo"][1])),
        ("mp_w1p0", w1p_l[0]), ("mp_w2p0", w2p_l[0]),
        ("mp_w1p1", w1p_l[1]), ("mp_w2p1", w2p_l[1]),
    ]
    sizes = [a.size for _, a in cast_list]
    total = sum(sizes)
    width = -(-total // (8 * 128 * 10)) * 10
    M = 128 * width
    blob = np.zeros(8 * M, np.float32)
    blob[:total] = np.concatenate([a.ravel() for _, a in cast_list])
    ncp = _prog("prep", lambda: build_prep(width))
    res = run8(ncp, [dict(fin=blob[c * M:(c + 1) * M].reshape(128, width))
                     for c in range(8)])
    bblob = np.concatenate([np.asarray(res.results[c]["fout"]).ravel()
                            for c in range(8)])
    cast = {}
    off = 0
    for (nm, a), sz in zip(cast_list, sizes):
        cast[nm] = np.ascontiguousarray(bblob[off:off + sz].reshape(a.shape))
        off += sz

    b_qkv = f32(I["L0_attn_a_b_qkv"])
    bqkvc = np.zeros((64, 48), np.float32)
    for w in range(3):
        for h in range(16):
            bqkvc[:, w * 16 + h] = b_qkv[1024 * w + 64 * h:1024 * w + 64 * (h + 1)]
    bout = _colmaj(I["L0_attn_a_b_out"], 8)
    nc = _prog("l0attna", build_l0attna)
    ims = []
    for c in range(8):
        xb = x0[c // 2]
        ofirst = xb if c % 2 == 0 else np.concatenate([xb[2048:], xb[:2048]], 0)
        ims.append(dict(xTb=np.ascontiguousarray(ofirst.T),
                        wqkvb=cast["wqkv_a"], woutb=cast["wout_a"],
                        bout=bout, bqkvc=bqkvc, idn=idn))
    res = run8(nc, ims)
    xc = [np.asarray(res.results[c]["yT"]) for c in range(8)]
    _tr("a1", xc)

    ffn_base = dict(w1b=cast["w1_w"], w2b=cast["w2_w"], h1b=cast["h1_w"],
                    h2b=cast["h2_w"], w3b=cast["w3_w"],
                    s1=_colmaj(I["L0_ffn_w1_s"], 32),
                    s2=_colmaj(I["L0_ffn_w2_s"], 32),
                    sh1=_colmaj(I["L0_ffn_h1_s"], 32),
                    sh2=_colmaj(I["L0_ffn_h2_s"], 32),
                    s3=_colmaj(I["L0_ffn_w3_s"], 8))

    def run_ffn(xc):
        nc = _prog("l0ffn", lambda: build_l0ffn(pre_t=True))
        res = run8(nc, [dict(xT=xc[c], **ffn_base) for c in range(8)])
        return [np.asarray(res.results[c]["yT"]) for c in range(8)]

    def run_ln(xc):
        nc = _prog("ln_only", build_ln_only)
        res = run8(nc, [dict(xT=xc[c]) for c in range(8)])
        return [np.asarray(res.results[c]["zT"]) for c in range(8)]

    xc = run_ffn(xc)
    _tr("f1", xc)

    zc = run_ln(xc)
    abc = l0attnb_consts()
    Wg, bg = f32(I["L0_attn_b_W_gamma"]), f32(I["L0_attn_b_b_gamma"])
    Wl, bl = f32(I["L0_attn_b_W_lambda"]), f32(I["L0_attn_b_b_lambda"])
    nc = _prog("l0attnb", build_l0attnb)
    ims = []
    for c in range(8):
        b, h = c // 2, c % 2
        ims.append(dict(
            zTb=np.ascontiguousarray(np.concatenate([zc[2 * b], zc[2 * b + 1]], 1)),
            wqT=cast[f"ab_q{h}"], wkT=cast[f"ab_k{h}"], wvT=cast[f"ab_v{h}"],
            woT=cast[f"ab_o{h}"],
            wgc=np.ascontiguousarray(Wg[h].reshape(4, 128).T),
            bgc=np.ascontiguousarray(bg[h].reshape(4, 128).T),
            wlc=np.ascontiguousarray(Wl[h].reshape(4, 128).T),
            blc=np.ascontiguousarray(bl[h].reshape(4, 128).T), **abc))
    res = run8(nc, ims)
    po = [np.asarray(res.results[c]["poT"]) for c in range(8)]

    nc = _prog("combine3", build_combine3)
    ims = []
    for c in range(8):
        b, sl = c // 2, c % 2
        ims.append(dict(xT=xc[c],
                        pa=np.ascontiguousarray(po[2 * b][:, 2048 * sl:2048 * (sl + 1)]),
                        pb=np.ascontiguousarray(po[2 * b + 1][:, 2048 * sl:2048 * (sl + 1)])))
    res = run8(nc, ims)
    xc = [np.asarray(res.results[c]["yT"]) for c in range(8)]
    _tr("c3", xc)

    xc = run_ffn(xc)
    _tr("f2", xc)

    nc = _prog("ln_tok", build_ln_tok)
    res = run8(nc, [dict(xT=xc[c], idn=idn) for c in range(8)])
    zc = [np.asarray(res.results[c]["zT"]) for c in range(8)]
    ztk = [np.asarray(res.results[c]["zTOK"]) for c in range(8)]

    bdm = np.zeros((128, 128), np.float32)
    for t in range(8):
        bdm[16 * t:16 * (t + 1), 16 * t:16 * (t + 1)] = 1.0
    aa_base = dict(sigr=f32(I["L1_attn_a_r_sigma"]).reshape(1, 1), idn=idn,
                   bdmaskh=bf(bdm),
                   negh_h=bf(np.full((1, 128), -0.5, np.float32)),
                   oneb64h=bf(np.ones((1, 128), np.float32)),
                   onec64h=bf(np.ones((128, 1), np.float32)),
                   oneb1h=bf(np.ones((1, 1), np.float32)))
    nc = _prog("l1attna", build_l1attna)
    res = run8(nc, [dict(zTK=ztk[c].reshape(16 * TLOC, 128), **aa_base)
                    for c in range(8)])
    yAs = [np.asarray(res.results[c]["yA"]) for c in range(8)]

    nc = _prog("combA", build_combA)
    res = run8(nc, [dict(xT=xc[c], zT=zc[c], yAc=yAs[c].reshape(TLOC, D))
                    for c in range(8)])
    xc = [np.asarray(res.results[c]["yTo"]) for c in range(8)]
    _tr("aa", xc)

    p32 = np.zeros((128, 128), np.float32)
    for b2_ in range(2):
        for r in range(32):
            p32[64 * b2_ + r + 32, 64 * b2_ + r] = -1.0
            p32[64 * b2_ + r, 64 * b2_ + r + 32] = 1.0
    bselh = np.zeros((128, 2), np.float32)
    bselh[0:64, 0] = 1
    bselh[64:128, 1] = 1
    reph = np.zeros((2, 128), np.float32)
    reph[0, 0:64] = 1
    reph[1, 64:128] = 1
    maskp = np.zeros((4, 128, 1024), np.float32)
    for kt in range(8):
        mi, hf = kt // 2, kt % 2
        for ik in range(128):
            lo = 128 * kt - 512 + ik
            hi = 128 * kt + ik
            jlo, jhi = max(0, lo), min(511, hi)
            if jlo <= jhi:
                maskp[mi, ik, 512 * hf + jlo:512 * hf + jhi + 1] = 1.0
    at_tabs = []
    for cp in range(2):
        t0 = 2048 * cp
        cq, sq_, scq = _rotary_tables(np.arange(t0, t0 + 2048,
                                                dtype=np.float32), n_ctx=4096)
        ck, sk_, sck = _rotary_tables(np.arange(t0 - 512, t0 + 2048,
                                                dtype=np.float32), n_ctx=4096)
        mk2 = lambda a: bf(np.tile(a.T, (2, 1)))
        at_tabs.append(dict(csqh=mk2(cq * scq), ssqh=mk2(sq_ * scq),
                            cskh=mk2(ck / sck), sskh=mk2(sk_ / sck)))

    def run_at(zcur, l):
        qs = f32(I["L1_ffn_q_scale"][l])
        ks = f32(I["L1_ffn_k_scale"][l])
        nc = _prog("l1attn", build_l1attn)
        ims = []
        for c in range(8):
            zin = np.zeros((D, 2688), np.float32)
            zin[:, 640:] = zcur[c]
            if c % 2 != 0:
                zin[:, 0:640] = zcur[c - 1][:, 2048 - 640:]
            ims.append(dict(
                zT=zin, wqkvl=cast[f"at_qkv{l}"], wol=cast[f"at_wo{l}"],
                qscol=np.tile(qs, 2).reshape(128, 1).astype(np.float32),
                kscol=np.tile(ks, 2).reshape(128, 1).astype(np.float32),
                qsrow=qs.reshape(1, 64), ksrow=ks.reshape(1, 64),
                p32h=bf(p32), bselhh=bf(bselh), rephh=reph, maskp=bf(maskp),
                killc=np.full((128, 1), 0.0 if c % 2 == 0 else 1.0, np.float32),
                **at_tabs[c % 2]))
        res = run8(nc, ims)
        return [np.asarray(res.results[c]["zoT"]) for c in range(8)]

    def run_mp(zcur, l, final, xres=None):
        lg = f32(I["L1_ffn_ln_g"][l])
        lbv = f32(I["L1_ffn_ln_b"][l])
        b1 = f32(I["L1_ffn_b1"][l])
        b1a = np.zeros((128, 22), np.float32)
        b1g = np.zeros((128, 22), np.float32)
        for i in range(22):
            n = min(128, HID - 128 * i)
            if n > 0:
                b1a[0:n, i] = b1[128 * i:128 * i + n]
                b1g[0:n, i] = b1[HID + 128 * i:HID + 128 * i + n]
        nc = _prog(("l1mlp", final), lambda: build_l1mlp(final))
        ims = []
        for c in range(8):
            zin = np.zeros((D, TLOC + 1), np.float32)
            zin[:, 1:] = zcur[c]
            if c % 2 != 0:
                zin[:, 0] = zcur[c - 1][:, -1]
            im = dict(zT=zin, w1p=cast[f"mp_w1p{l}"], b1a=b1a, b1g=b1g,
                      lng=_colmaj(lg, 8), lnb=_colmaj(lbv, 8),
                      w2p=cast[f"mp_w2p{l}"], b2c=_colmaj(f32(I["L1_ffn_b2"][l]), 8))
            if final:
                im["xT"] = xres[c]
            ims.append(im)
        res = run8(nc, ims)
        return [np.asarray(res.results[c]["zoT"]) for c in range(8)]

    def run_l1ffn(xcur, tag=""):
        z = run_ln(xcur)
        z = run_at(z, 0)
        _tr("at0" + tag, z)
        z = run_mp(z, 0, False)
        _tr("mp0" + tag, z)
        z = run_at(z, 1)
        _tr("at1" + tag, z)
        return run_mp(z, 1, True, xres=xcur)

    xc = run_l1ffn(xc)
    _tr("ffnA", xc)

    q_g, k_g = f32(I["L1_attn_b_q_g"]), f32(I["L1_attn_b_k_g"])
    kgcol = np.zeros((128, 1), np.float32)
    kgcol[0:32, 0] = k_g
    kgcol[64:96, 0] = k_g
    bsel_q = np.zeros((128, 2), np.float32)
    bsel_q[0:64, 0] = 1
    bsel_q[64:128, 1] = 1
    bsel_k = np.zeros((128, 2), np.float32)
    bsel_k[0:32, 0] = 1
    bsel_k[64:96, 1] = 1
    rep_q = np.zeros((2, 128), np.float32)
    rep_q[0, 0:64] = 1
    rep_q[1, 64:128] = 1
    rep_k = np.zeros((2, 128), np.float32)
    rep_k[0, 0:32] = 1
    rep_k[1, 64:96] = 1
    masks = np.zeros((4, 128, 512), np.float32)
    for mi in range(4):
        for ik in range(128):
            masks[mi, ik, 128 * mi + ik:] = 1.0
    bb_base = dict(wqb=cast["bb_q"], wkvb=cast["bb_kv"], wkbp=cast["bb_kbp"],
                   wvb=cast["bb_v"], wob=cast["bb_ob"],
                   qgcol=np.tile(q_g, 2).reshape(128, 1).astype(np.float32),
                   kgcol=kgcol, bsel_q=bf(bsel_q), rep_q=rep_q,
                   bsel_k=bf(bsel_k), rep_k=rep_k,
                   qgrow=q_g.reshape(1, 64), kgrow=k_g.reshape(1, 32),
                   masksh=bf(masks))
    xb6 = [np.ascontiguousarray(np.concatenate([xc[2 * b], xc[2 * b + 1]], 1))
           for b in range(4)]
    x7 = [None] * 8
    for q0 in (0, 2048):
        nc = _prog(("l1attnb", q0), lambda: build_l1attnb(q0))
        ims = [dict(xTb=xb6[b % 4], **bb_base) for b in range(8)]
        res = run8(nc, ims)
        for b in range(4):
            x7[2 * b + (q0 // 2048)] = np.asarray(res.results[b]["yT"])
    xc = x7
    _tr("bb", xc)

    xc = run_l1ffn(xc, "B")
    _tr("ffnB", xc)

    out = np.concatenate([xc[c].T for c in range(8)], 0)
    return np.ascontiguousarray(out.reshape(4, 4096, 1024), dtype=np.float32)
